# version 04
# baseline (speedup 1.0000x reference)
import os
import numpy as np
import concourse.bass as bass
import concourse.mybir as mybir
from concourse.bass_utils import run_bass_kernel_spmd

F32 = mybir.dt.float32
BF16 = mybir.dt.bfloat16
AF = mybir.ActivationFunctionType
ALU = mybir.AluOpType
AX = mybir.AxisListType.X

D = 2048
ND = 16
SLOTS = 2048
NQ = 4
NE = 16
NI = 11
NSH = 22
NJC = 96
EPS = 1e-5
POOL_N = 28672
PB = 20480


class Em:
    ENG = ("sy", "gp", "te", "ve", "se")
    SEMS = ("d", "x", "t", "v", "s")

    def __init__(self):
        self.ops = {e: [] for e in self.ENG}
        self.cnt = {s: 0 for s in self.SEMS}
        self.hw = {e: {s: 0 for s in self.SEMS} for e in self.ENG}
        self.enabled = True

    def w(self, eng, ev):
        if not self.enabled:
            return
        if ev is None:
            return
        s, val = ev
        if self.hw[eng][s] >= val:
            return
        self.hw[eng][s] = val
        self.ops[eng].append(("w", s, val))

    def op(self, eng, fn, inc=None, amt=1):
        if not self.enabled:
            return None
        ev = None
        if inc is not None:
            self.cnt[inc] += amt
            ev = (inc, self.cnt[inc])
        self.ops[eng].append(("o", fn, inc, amt))
        return ev


def build_nc(upto=7):
    nc = bass.Bass("TRN2", target_bir_lowering=False)

    dXF = nc.dram_tensor("xf", [D, SLOTS], F32, kind="ExternalInput").ap()
    dWin = nc.dram_tensor("win", [32, 128, 16, 128], F32, kind="ExternalInput").ap()
    dWmod = nc.dram_tensor("wmod", [NJC, 128, 16, 128], F32, kind="ExternalInput").ap()
    dW1 = nc.dram_tensor("w1", [NE, NI, 128, 16, 128], F32, kind="ExternalInput").ap()
    dW3 = nc.dram_tensor("w3", [NE, NI, 128, 16, 128], F32, kind="ExternalInput").ap()
    dW2 = nc.dram_tensor("w2", [NE, ND, 128, NI, 128], F32, kind="ExternalInput").ap()
    dWs1 = nc.dram_tensor("ws1", [NSH, 128, 16, 128], F32, kind="ExternalInput").ap()
    dWs3 = nc.dram_tensor("ws3", [NSH, 128, 16, 128], F32, kind="ExternalInput").ap()
    dWs2 = nc.dram_tensor("ws2", [ND, 128, NSH, 128], F32, kind="ExternalInput").ap()
    dGW = nc.dram_tensor("gw", [128, 16, 16], F32, kind="ExternalInput").ap()
    dC = nc.dram_tensor("cvec", [128, 16], F32, kind="ExternalInput").ap()
    dAP = nc.dram_tensor("apcol", [128, 1], F32, kind="ExternalInput").ap()
    dMC = nc.dram_tensor("mcol", [128, 1], F32, kind="ExternalInput").ap()
    dID = nc.dram_tensor("ident", [128, 128], F32, kind="ExternalInput").ap()
    dSEL = nc.dram_tensor("sels", [16, 2048], F32, kind="ExternalInput").ap()
    dOUT = nc.dram_tensor("out", [D, 1024], F32, kind="ExternalOutput").ap()
    K_DBG = os.environ.get("K_DBG") == "1"
    dDBG = nc.dram_tensor("dbg", [128, 16384], F32, kind="ExternalOutput").ap() if K_DBG else None

    em = Em()
    from contextlib import ExitStack
    with ExitStack() as ctx:
        E = ctx.enter_context
        pool = E(nc.sbuf_tensor("pool", [128, POOL_N], F32))
        X1 = E(nc.sbuf_tensor("x1", [128, ND * 1024], F32))
        BP = E(nc.sbuf_tensor("bp", [128, 2048], F32))
        BQ = E(nc.sbuf_tensor("bq", [128, 2048], F32))
        ident = E(nc.sbuf_tensor("identsb", [128, 128], F32))
        sels = E(nc.sbuf_tensor("selssb", [16, 2048], F32))
        modv = E(nc.sbuf_tensor("modv", [128, 96], F32))
        modx = E(nc.sbuf_tensor("modx", [128, 96], F32))
        lnw = E(nc.sbuf_tensor("lnw", [128, 64], F32))
        lnw2 = E(nc.sbuf_tensor("lnw2", [128, 32], F32))
        ssum = E(nc.sbuf_tensor("ssum", [128, 32], F32))
        TKR2 = E(nc.sbuf_tensor("tkr2", [128, 8], F32))
        PTs = E(nc.sbuf_tensor("pts", [16, 128], F32))
        QTs = E(nc.sbuf_tensor("qts", [16, 128], F32))
        onescol = E(nc.sbuf_tensor("onescol", [128, 1], F32))
        padv = E(nc.sbuf_tensor("padv", [128, 448], F32))
        pads = E(nc.sbuf_tensor("pads", [128, 384], F32))
        epscol = E(nc.sbuf_tensor("epscol", [128, 1], F32))
        stcol = E(nc.sbuf_tensor("stcol", [128, 1], F32))
        mcol = E(nc.sbuf_tensor("mcolsb", [128, 1], F32))
        apm4 = E(nc.sbuf_tensor("apm4", [128, 1], F32))
        apsp = E(nc.sbuf_tensor("apsp", [128, 1], F32))
        apraw = E(nc.sbuf_tensor("apraw", [128, 1], F32))
        cvec = E(nc.sbuf_tensor("cvecsb", [128, 16], F32))
        siluc = E(nc.sbuf_tensor("siluc", [128, 16], F32))
        scolb = E(nc.sbuf_tensor("scolb", [128, 16], BF16))
        gw = E(nc.sbuf_tensor("gwb", [128, 256], BF16))
        PS = [E(nc.psum_tensor(f"ps{i}", [128, 1024], F32)) for i in range(4)]
        sem = {s: E(nc.semaphore(f"sem_{s}")) for s in Em.SEMS}
        block = E(nc.Block())

        def pv(off, n, dt=F32):
            apx = pool[:, off:off + n]
            return apx if dt is F32 else apx.bitcast(dt)

        Hbf = pv(0, 16384, BF16)
        WINS = [pv(16384 + 1024 * i, 1024, BF16) for i in range(3)]
        WMS = [pv(19456, 1024, BF16), pv(PB + 6144, 1024, BF16)]
        xb = [pv(PB, 2048), pv(PB + 2048, 2048)]
        Txx = pv(PB + 4096, 2048)
        U1f = pv(PB + 4096, 2048)
        U2f = pv(PB + 6144, 2048)
        TxQ = [pv(PB, 512), pv(PB + 512, 512)]
        U1q = pv(PB + 1024, 512)
        U2q = pv(PB + 1536, 512)
        Tgx = [pv(PB + 2048, 512), pv(PB + 2560, 512)]
        Tla = [pv(PB + 3072, 512), pv(PB + 3584, 512)]
        Ta = [pv(PB + 4096, 512), pv(PB + 4608, 512)]
        Tmu = [pv(PB + 5120, 512), pv(PB + 5632, 512)]
        Hbf2 = pv(0, 8192, BF16)
        Gbf = pv(8192, 5632, BF16)
        W13S = [pv(13824 + 1024 * i, 1024, BF16) for i in range(4)]
        W2S = [pv(17920, 704, BF16), pv(18624, 704, BF16)]
        CBC = [pv(19328, 1024), pv(20352, 1024)]
        Tsil = pv(21376, 1024)
        Tgf = pv(22400, 1024)
        Fu1 = pv(23424, 1024)
        Fu2 = pv(24448, 1024)
        Fxx = pv(25472, 1024)
        scoresTM = pv(25472, 128)
        combTM = pv(25600, 128)
        TK = pv(25728, 768)
        TKR = pv(27520, 64)
        scoresF = pv(26496, 1024)
        combFM = pv(27648, 1024)
        Zbf = pv(8192, 11264, BF16)
        WS1S = [pv(19456, 1024, BF16), pv(20480, 1024, BF16)]
        WS3S = [pv(21504, 1024, BF16), pv(22528, 1024, BF16)]
        WS2S = [pv(23552, 1408, BF16), pv(24960, 1408, BF16)]
        ZTsil = pv(26368, 1024)

        def HbfV(k):
            return Hbf[:, k * 2048:(k + 1) * 2048]

        def Hbf2V(k):
            return Hbf2[:, k * 1024:(k + 1) * 1024]

        def X1V(k):
            return X1[:, k * 1024:(k + 1) * 1024]

        def GbfV(ic):
            return Gbf[:, ic * 1024:(ic + 1) * 1024]

        def ZbfV(sc):
            return Zbf[:, sc * 1024:(sc + 1) * 1024]

        def vpad():
            em.op("ve", lambda e: e.tensor_copy(padv[:, :], X1[:, 0:448]))

        def vpinc():
            return em.op("ve", lambda e: e.tensor_copy(padv[:, :], X1[:, 0:448]), "v")

        def spinc():
            return em.op("se", lambda e: e.copy(pads[:, :], X1[:, 0:384]), "s")

        ev_c = em.op("sy", lambda e: e.dma_start(out=cvec[:, :], in_=dC[:, :]), "x", 16)
        ev_ap = em.op("sy", lambda e: e.dma_start(out=apraw[:, :], in_=dAP[:, :]), "x", 16)
        ev_mc = em.op("sy", lambda e: e.dma_start(out=mcol[:, :], in_=dMC[:, :]), "x", 16)
        ev_id = em.op("sy", lambda e: e.dma_start(out=ident[:, :], in_=dID[:, :]), "x", 16)
        ev_sel = em.op("sy", lambda e: e.dma_start(out=sels[:, :], in_=dSEL[:, :]), "x", 16)
        ev_gw = em.op("gp", lambda e: e.dma_start(
            out=gw[:, :].rearrange("p (a b) -> p a b", a=16), in_=dGW[:, :, :]), "d", 16)

        em.op("ve", lambda e: e.memset(onescol[:, :], 1.0))
        em.op("ve", lambda e: e.memset(epscol[:, :], EPS))
        ev_ones = vpinc()
        ev_eps = ev_ones
        em.w("se", ev_c)
        em.op("se", lambda e: e.activation(siluc[:, :], cvec[:, :], AF.Silu))
        ev_sc = spinc()
        em.w("ve", ev_sc)
        em.op("ve", lambda e: e.tensor_copy(scolb[:, :], siluc[:, :]))
        ev_scb = vpinc()
        em.w("se", ev_ap)
        em.op("se", lambda e: e.activation(apsp[:, :], apraw[:, :], AF.Exp))
        ev_ex = spinc()
        em.w("ve", ev_ex)
        em.op("ve", lambda e: e.tensor_scalar(
            apraw[:, :], apsp[:, :], 1.0, None, ALU.add))
        ev_p1 = vpinc()
        em.w("se", ev_p1)
        em.op("se", lambda e: e.activation(apsp[:, :], apraw[:, :], AF.Ln))
        ev_sp = spinc()
        em.w("ve", ev_sp)
        em.op("ve", lambda e: e.tensor_scalar(
            apm4[:, :], apsp[:, :], -4.0, None, ALU.mult))
        ev_ap4 = vpinc()

        wm_load = [None] * NJC
        wm_mm = [None] * NJC

        def emit_wmod_dma(jc, extra_waits=()):
            for wv in extra_waits:
                em.w("gp", wv)
            if jc >= 2:
                em.w("gp", wm_mm[jc - 2])
            wm_load[jc] = em.op(
                "gp", lambda e, jc=jc: e.dma_start(
                    out=WMS[jc % 2][:, :].rearrange("p (a b) -> p a b", a=16),
                    in_=dWmod[jc]), "d", 16)

        def emit_wmod_mm(jc, extra_waits=()):
            em.w("te", wm_load[jc])
            em.w("te", ev_scb)
            for wv in extra_waits:
                em.w("te", wv)

            def mmfn(e, jc=jc):
                r = None
                for dc in range(16):
                    r = e.matmul(PS[3][:, jc:jc + 1],
                                 WMS[jc % 2][:, dc * 128:(dc + 1) * 128],
                                 scolb[:, dc:dc + 1],
                                 start=(dc == 0), stop=(dc == 15))
                return r
            wm_mm[jc] = em.op("te", mmfn, "t")

        K_WMOD = os.environ.get("K_WMOD") == "1"
        K_STATS = os.environ.get("K_STATS") == "1"
        xa_load = [None] * ND
        xa_sq = [None] * ND
        xa_st = [None] * ND
        for k in range(ND):
            if not K_WMOD:
                for j in (2 * k, 2 * k + 1):
                    emit_wmod_dma(j)
            if k >= 2:
                em.w("sy", xa_st[k - 2])
                em.w("sy", xa_sq[k - 2])
            if not K_STATS:
                xa_load[k] = em.op(
                    "sy", lambda e, k=k: e.dma_start(
                        out=xb[k % 2][:, :], in_=dXF[k * 128:(k + 1) * 128, :]), "x", 16)
                em.w("ve", xa_load[k])
                if k >= 1:
                    em.w("ve", xa_st[k - 1])
                xa_sq[k] = em.op(
                    "ve", lambda e, k=k: e.tensor_tensor(
                        Txx[:, :], xb[k % 2][:, :], xb[k % 2][:, :], ALU.mult), "v")
            if not K_WMOD:
                for j in (2 * k, 2 * k + 1):
                    emit_wmod_mm(j)
            if K_STATS:
                continue
            em.w("te", xa_load[k])
            em.w("te", xa_sq[k])
            em.w("te", ev_ones)

            def stfn(e, k=k):
                r = None
                for tc in range(16):
                    e.matmul(PS[2][:, k * 16 + tc:k * 16 + tc + 1],
                             xb[k % 2][:, tc * 128:(tc + 1) * 128], onescol[:, :],
                             start=True, stop=True)
                for tc in range(16):
                    r = e.matmul(PS[2][:, 256 + k * 16 + tc:256 + k * 16 + tc + 1],
                                 Txx[:, tc * 128:(tc + 1) * 128], onescol[:, :],
                                 start=True, stop=True)
                return r
            xa_st[k] = em.op("te", stfn, "t")

        if K_WMOD:
            wm_mm = [None] * NJC
        em.w("ve", wm_mm[31])
        em.op("ve", lambda e: e.tensor_copy(modv[:, 0:32], PS[3][:, 0:32]))
        ev_pmh = vpinc()
        em.w("ve", ev_mc)
        em.op("ve", lambda e: e.tensor_scalar(
            modx[:, 0:16], modv[:, 16:32], 1.0, 1.0, ALU.mult, ALU.add))
        vpad()
        em.op("ve", lambda e: e.tensor_scalar(
            modx[:, 64:80], modx[:, 0:16], mcol[:, :], None, ALU.mult))
        em.op("ve", lambda e: e.tensor_scalar(
            modx[:, 80:96], modv[:, 0:16], mcol[:, :], None, ALU.mult))
        ev_modh = vpinc()

        def emit_ln(ntc, ev_stats):
            n = float(D)
            em.w("ve", ev_stats)
            em.op("ve", lambda e: e.tensor_scalar(
                lnw[:, 0:ntc], ssum[:, 0:ntc], 1.0 / n, None, ALU.mult))
            em.op("ve", lambda e: e.tensor_scalar(
                lnw[:, 16:16 + ntc], ssum[:, 16:16 + ntc], 1.0 / n, None, ALU.mult))
            vpad()
            em.op("ve", lambda e: e.tensor_tensor(
                lnw[:, 32:32 + ntc], lnw[:, 0:ntc], lnw[:, 0:ntc], ALU.mult))
            vpad()
            em.op("ve", lambda e: e.tensor_tensor(
                lnw[:, 16:16 + ntc], lnw[:, 16:16 + ntc], lnw[:, 32:32 + ntc],
                ALU.subtract))
            ev_var = vpinc()
            if K_DBG and ntc == 16:
                em.w("sy", ev_var)
                ev_dv = em.op("sy", lambda e: e.dma_start(
                    out=dDBG[0:128, 576:640], in_=lnw[:, :]), "x", 16)
                em.w("se", ev_dv)
            em.w("se", ev_var)
            em.w("se", ev_eps)
            em.op("se", lambda e: e.activation(
                lnw[:, 32:32 + ntc], lnw[:, 16:16 + ntc], AF.Sqrt, bias=epscol[:, :]))
            ev_std = spinc()
            em.w("ve", ev_std)
            em.op("ve", lambda e: e.reciprocal(lnw2[:, 0:ntc], lnw[:, 32:32 + ntc]))
            vpad()
            em.op("ve", lambda e: e.tensor_tensor(
                lnw2[:, 16:16 + ntc], lnw[:, 0:ntc], lnw2[:, 0:ntc], ALU.mult))
            ev_pq = vpinc()
            em.w("te", ev_pq)
            em.w("te", ev_id)
            em.op("te", lambda e: e.transpose(
                PS[1][0:ntc, 0:128], lnw2[:, 0:ntc], ident[:, :]), "t")
            ev_tq = em.op("te", lambda e: e.transpose(
                PS[1][0:ntc, 128:256], lnw2[:, 16:16 + ntc], ident[:, :]), "t")
            em.w("se", ev_tq)
            em.op("se", lambda e: e.copy(PTs[0:ntc, :], PS[1][0:ntc, 0:128]))
            em.op("se", lambda e: e.copy(QTs[0:ntc, :], PS[1][0:ntc, 128:256]))
            ev_eq = spinc()
            if K_DBG and ntc == 8:
                em.w("sy", ev_eq)
                em.op("sy", lambda e: e.dma_start(
                    out=dDBG[0:128, 3136:3200], in_=lnw[:, :]), "x", 16)
                em.op("sy", lambda e: e.dma_start(
                    out=dDBG[0:128, 3200:3232], in_=lnw2[:, :]), "x", 16)
                em.op("sy", lambda e: e.dma_start(
                    out=dDBG[0:16, 3232:3360], in_=PTs[0:16, :]), "x", 16)
                ev_dd = em.op("sy", lambda e: e.dma_start(
                    out=dDBG[0:16, 3360:3488], in_=QTs[0:16, :]), "x", 16)
                em.w("te", ev_dd)
                em.w("ve", ev_dd)
            em.w("te", ev_eq)
            em.w("te", ev_sel)
            done = None
            gdone = [None] * (ntc // 4)
            for g in range(ntc // 4):
                pst = PS[0] if g % 2 == 0 else PS[1]
                if g >= 2:
                    em.w("te", gdone[g - 2])

                def bcf(e, g=g, pst=pst):
                    r = None
                    for i in range(4):
                        c = 4 * g + i
                        e.matmul(pst[:, i * 128:(i + 1) * 128],
                                 sels[:, c * 128:(c + 1) * 128], PTs[0:16, :],
                                 start=True, stop=True)
                    for i in range(4):
                        c = 4 * g + i
                        r = e.matmul(pst[:, 512 + i * 128:512 + (i + 1) * 128],
                                     sels[:, c * 128:(c + 1) * 128], QTs[0:16, :],
                                     start=True, stop=True)
                    return r
                evb = em.op("te", bcf, "t")
                em.w("ve", evb)
                em.op("ve", lambda e, g=g, pst=pst: e.tensor_copy(
                    BP[:, g * 512:(g + 1) * 512], pst[:, 0:512]))
                done = em.op("ve", lambda e, g=g, pst=pst: e.tensor_copy(
                    BQ[:, g * 512:(g + 1) * 512], pst[:, 512:1024]), "v")
                gdone[g] = done
            return done

        em.w("ve", xa_st[ND - 1])
        if not K_STATS:
            em.op("ve", lambda e: e.reduce_sum(
                ssum[:, 0:16], PS[2][:, 0:256].rearrange("p (a b) -> p b a", a=16), axis=AX))
        ev_red1 = em.op("ve", lambda e: (e.reduce_sum(
            ssum[:, 16:32], PS[2][:, 256:512].rearrange("p (a b) -> p b a", a=16), axis=AX)
            if not K_STATS else e.memset(ssum[:, :], 1.0)), "v")
        em.enabled = upto >= 2
        ev_bpq1 = emit_ln(16, ev_red1)
        if K_DBG:
            em.w("sy", ev_bpq1)
            em.w("sy", ev_modh)
            em.op("sy", lambda e: e.dma_start(out=dDBG[0:128, 0:96], in_=modv[:, :]), "x", 16)
            em.op("sy", lambda e: e.dma_start(out=dDBG[0:128, 96:192], in_=modx[:, :]), "x", 16)
            em.op("sy", lambda e: e.dma_start(out=dDBG[0:128, 192:224], in_=ssum[:, :]), "x", 16)
            em.op("sy", lambda e: e.dma_start(out=dDBG[0:128, 224:288], in_=lnw[:, :]), "x", 16)
            em.op("sy", lambda e: e.dma_start(out=dDBG[0:128, 288:320], in_=lnw2[:, :]), "x", 16)
            em.op("sy", lambda e: e.dma_start(out=dDBG[0:16, 320:448], in_=PTs[:, :]), "x", 16)
            em.op("sy", lambda e: e.dma_start(out=dDBG[0:16, 448:576], in_=QTs[:, :]), "x", 16)
            em.op("sy", lambda e: e.dma_start(out=dDBG[0:128, 1024:3072], in_=BP[:, :]), "x", 16)
            em.op("sy", lambda e: e.dma_start(out=dDBG[0:128, 3072:5120], in_=BQ[:, :]), "x", 16)

        xb_load = [None] * ND
        hb_done = [None] * ND
        for k in range(ND):
            if k >= 2:
                em.w("sy", hb_done[k - 2])
            else:
                em.w("sy", xa_st[ND - 2 + k])
                em.w("sy", xa_sq[ND - 2 + k])
            xb_load[k] = em.op(
                "sy", lambda e, k=k: e.dma_start(
                    out=xb[k % 2][:, :], in_=dXF[k * 128:(k + 1) * 128, :]), "x", 16)
            em.w("ve", xb_load[k])
            em.w("ve", ev_bpq1)
            em.w("ve", ev_modh)
            em.op("ve", lambda e, k=k: e.tensor_tensor(
                U1f[:, :], xb[k % 2][:, :], BP[:, :], ALU.mult))
            em.op("ve", lambda e: e.tensor_tensor(
                U2f[:, :], U1f[:, :], BQ[:, :], ALU.subtract))
            hb_done[k] = em.op("ve", lambda e, k=k: e.tensor_scalar(
                HbfV(k), U2f[:, :], modx[:, 0 + k:1 + k], modv[:, 0 + k:1 + k],
                ALU.mult, ALU.add), "v")
        ev_hbf_all = hb_done[ND - 1]
        ev_passB_dead = hb_done[ND - 1]
        if K_DBG:
            em.w("ve", ev_hbf_all)
            ev_hd = em.op("ve", lambda e: e.tensor_copy(U1f[:, :], HbfV(0)), "v")
            em.w("sy", ev_hd)
            ev_hd2 = em.op("sy", lambda e: e.dma_start(
                out=dDBG[0:128, 5120:7168], in_=U1f[:, :]), "x", 16)
            em.w("ve", ev_hd2)

        em.enabled = upto >= 3
        win_load = [None] * 32
        win_used = [None] * 32
        ev_unit_tanh = [None] * 64
        ev_unit_mm = [None] * 64
        scan_ev = [None] * 64
        txq_use = [None] * 64
        ev_tail_evac = [None] * ND
        se_last = None

        for k in range(ND):
            tail_waits = [ev_pmh] if k == 0 else [ev_tail_evac[k - 1]]
            for i in range(2):
                jc = 32 + 4 * k + i
                extra_d = [ev_passB_dead] if (jc % 2 == 1) else []
                emit_wmod_dma(jc, extra_waits=extra_d)
            for g in range(2):
                i = 2 * k + g
                if i >= 3:
                    em.w("gp", win_used[i - 3])
                oc = g * 16 + k
                win_load[i] = em.op(
                    "gp", lambda e, i=i, oc=oc: e.dma_start(
                        out=WINS[i % 3][:, :].rearrange("p (a b) -> p a b", a=16),
                        in_=dWin[oc]), "d", 16)
            for i in range(2):
                jc = 32 + 4 * k + i
                emit_wmod_mm(jc, extra_waits=(tail_waits if i == 0 else []))
            for i in range(2, 4):
                jc = 32 + 4 * k + i
                extra_d = [ev_passB_dead] if (jc % 2 == 1) else []
                emit_wmod_dma(jc, extra_waits=extra_d)
            for i in range(2, 4):
                jc = 32 + 4 * k + i
                emit_wmod_mm(jc)
            em.w("ve", wm_mm[32 + 4 * k + 3])
            em.op("ve", lambda e, k=k: e.tensor_copy(
                modv[:, 32 + 4 * k:36 + 4 * k], PS[3][:, 32 + 4 * k:36 + 4 * k]))
            ev_tail_evac[k] = vpinc()
            if k < 4:
                c0 = 32 + 4 * k
                o0 = 16 + 4 * k
                em.op("ve", lambda e, c0=c0: e.tensor_scalar(
                    lnw[:, 56:60], modv[:, c0:c0 + 4], -1.0, None, ALU.mult))
                em.op("ve", lambda e, c0=c0: e.tensor_scalar(
                    lnw[:, 48:52], modv[:, c0:c0 + 4], 0.0, None, ALU.is_ge))
                vpad()
                em.op("ve", lambda e, c0=c0, o0=o0: e.tensor_tensor(
                    modx[:, o0:o0 + 4], modv[:, c0:c0 + 4], lnw[:, 56:60], ALU.max))
                em.op("ve", lambda e: e.tensor_scalar(
                    lnw[:, 52:56], lnw[:, 48:52], 2.0, -1.0, ALU.mult, ALU.add))
                vpad()
                em.op("ve", lambda e, o0=o0: e.tensor_scalar(
                    modx[:, o0:o0 + 4], modx[:, o0:o0 + 4], 0.25, None, ALU.max))
                vpad()
                em.op("ve", lambda e, o0=o0: e.tensor_tensor(
                    modx[:, o0:o0 + 4], modx[:, o0:o0 + 4], lnw[:, 52:56], ALU.mult))
                vpad()

            for q in range(NQ):
                u = 4 * k + q
                ps = PS[u % 3]
                st = q % 2
                em.w("te", ev_hbf_all)
                em.w("te", ev_bpq1)
                em.w("te", win_load[2 * k])
                em.w("te", win_load[2 * k + 1])
                if u >= 3:
                    em.w("te", ev_unit_tanh[u - 3])

                def gmm(e, k=k, q=q, ps=ps):
                    r = None
                    for kk in range(ND):
                        e.matmul(ps[:, 0:512],
                                 WINS[(2 * k) % 3][:, kk * 128:(kk + 1) * 128],
                                 HbfV(kk)[:, q * 512:(q + 1) * 512],
                                 start=(kk == 0), stop=(kk == ND - 1))
                    for kk in range(ND):
                        r = e.matmul(ps[:, 512:1024],
                                     WINS[(2 * k + 1) % 3][:, kk * 128:(kk + 1) * 128],
                                     HbfV(kk)[:, q * 512:(q + 1) * 512],
                                     start=(kk == 0), stop=(kk == ND - 1))
                    return r
                ev_unit_mm[u] = em.op("te", gmm, "t")
                if q == NQ - 1:
                    win_used[2 * k] = ev_unit_mm[u]
                    win_used[2 * k + 1] = ev_unit_mm[u]
                em.w("se", ev_unit_mm[u])
                if u >= 2:
                    em.w("se", scan_ev[u - 2])
                em.op("se", lambda e, ps=ps, st=st: e.activation(
                    Tgx[st][:, :], ps[:, 0:512], AF.Tanh, scale=0.5))
                ev_unit_tanh[u] = em.op("se", lambda e, ps=ps, st=st: e.activation(
                    Tla[st][:, :], ps[:, 512:1024], AF.Tanh, scale=0.5), "s")
                em.w("ve", ev_unit_tanh[u])
                em.w("ve", ev_ap4)
                ev_la = em.op("ve", lambda e, st=st: e.tensor_scalar(
                    Tla[st][:, :], Tla[st][:, :], apm4[:, :], apm4[:, :],
                    ALU.mult, ALU.add), "v")
                em.op("ve", lambda e, st=st: e.tensor_scalar(
                    Tgx[st][:, :], Tgx[st][:, :], 0.5, 0.5, ALU.mult, ALU.add))
                em.w("se", ev_la)
                em.op("se", lambda e, st=st: e.activation(
                    Ta[st][:, :], Tla[st][:, :], AF.Exp))
                ev_e2 = em.op("se", lambda e, st=st: e.activation(
                    Tmu[st][:, :], Ta[st][:, :], AF.Square), "s")
                em.w("ve", ev_e2)
                ev_om = em.op("ve", lambda e, st=st: e.tensor_scalar(
                    Tmu[st][:, :], Tmu[st][:, :], -1.0, 1.0, ALU.mult, ALU.add), "v")
                em.w("se", ev_om)
                ev_mu = em.op("se", lambda e, st=st: e.activation(
                    Tla[st][:, :], Tmu[st][:, :], AF.Sqrt), "s")
                se_last = ev_mu
                if u >= 2:
                    em.w("sy", txq_use[u - 2])
                else:
                    em.w("sy", ev_hbf_all)
                ev_xq = em.op("sy", lambda e, k=k, q=q, u=u: e.dma_start(
                    out=TxQ[u % 2][:, :],
                    in_=dXF[k * 128:(k + 1) * 128, q * 512:(q + 1) * 512]), "x", 16)
                em.w("ve", ev_xq)
                em.op("ve", lambda e, u=u, q=q: e.tensor_tensor(
                    U1q[:, :], TxQ[u % 2][:, :], BP[:, q * 512:(q + 1) * 512], ALU.mult))
                em.op("ve", lambda e, q=q: e.tensor_tensor(
                    U2q[:, :], U1q[:, :], BQ[:, q * 512:(q + 1) * 512], ALU.subtract))
                if q < 2:
                    ca_ap, sa_ap = modx[:, 64 + k:65 + k], modx[:, 80 + k:81 + k]
                else:
                    ca_ap, sa_ap = modx[:, 0 + k:1 + k], modv[:, 0 + k:1 + k]
                em.op("ve", lambda e, ca_ap=ca_ap, sa_ap=sa_ap: e.tensor_scalar(
                    U1q[:, :], U2q[:, :], ca_ap, sa_ap, ALU.mult, ALU.add))
                em.op("ve", lambda e, st=st: e.tensor_tensor(
                    U2q[:, :], U1q[:, :], Tgx[st][:, :], ALU.mult))
                em.w("ve", ev_mu)
                em.op("ve", lambda e, st=st: e.tensor_tensor(
                    U1q[:, :], U2q[:, :], Tla[st][:, :], ALU.mult))
                init = 0.0 if q == 0 else stcol[:, :]
                if os.environ.get("K_SCAN") == "1":
                    scan_ev[u] = em.op("ve", lambda e, st=st: e.tensor_copy(
                        U2q[:, :], U1q[:, :]), "v")
                else:
                    scan_ev[u] = em.op("ve", lambda e, st=st, init=init: e.tensor_tensor_scan(
                        U2q[:, :], Ta[st][:, :], U1q[:, :], init, ALU.mult, ALU.add), "v")
                if K_DBG and u in (0, 2):
                    base = 7168 if u == 0 else 11264
                    em.w("sy", scan_ev[u])
                    for i2, src_ap in enumerate(
                            [Tgx[st], Tla[st], Ta[st], U1q, U2q, TxQ[u % 2]]):
                        ev_dd = em.op("sy", lambda e, i2=i2, base=base, src_ap=src_ap:
                                      e.dma_start(out=dDBG[0:128, base + i2 * 512:
                                                           base + (i2 + 1) * 512],
                                                  in_=src_ap[:, :]), "x", 16)
                    em.w("ve", ev_dd)
                    em.w("se", ev_dd)
                em.op("ve", lambda e: e.tensor_copy(stcol[:, :], U2q[:, 511:512]))
                if q >= 2:
                    txq_use[u] = em.op("ve", lambda e, k=k, q=q, u=u: e.scalar_tensor_tensor(
                        X1V(k)[:, (q - 2) * 512:(q - 1) * 512],
                        U2q[:, :], modx[:, 16 + k:17 + k], TxQ[u % 2][:, :],
                        ALU.mult, ALU.add), "v")
                else:
                    txq_use[u] = scan_ev[u]

        ev_passC_ve = txq_use[63]
        ev_passC_te = ev_unit_mm[63]
        ev_passC_se = se_last
        ev_tail_all = ev_tail_evac[ND - 1]
        ffn_gate = [ev_passC_ve, ev_passC_te, ev_passC_se, ev_tail_all]

        em.enabled = upto >= 4
        for evg in ffn_gate:
            em.w("ve", evg)
        em.op("ve", lambda e: e.tensor_scalar(
            modx[:, 32:48], modv[:, 64:80], 1.0, 1.0, ALU.mult, ALU.add))
        em.op("ve", lambda e: e.tensor_scalar(
            lnw[:, 0:16], modv[:, 80:96], -1.0, None, ALU.mult))
        em.op("ve", lambda e: e.tensor_scalar(
            lnw[:, 48:64], modv[:, 80:96], 0.0, None, ALU.is_ge))
        vpad()
        em.op("ve", lambda e: e.tensor_tensor(
            modx[:, 48:64], modv[:, 80:96], lnw[:, 0:16], ALU.max))
        em.op("ve", lambda e: e.tensor_scalar(
            lnw[:, 48:64], lnw[:, 48:64], 2.0, -1.0, ALU.mult, ALU.add))
        vpad()
        em.op("ve", lambda e: e.tensor_scalar(
            modx[:, 48:64], modx[:, 48:64], 0.25, None, ALU.max))
        vpad()
        em.op("ve", lambda e: e.tensor_tensor(
            modx[:, 48:64], modx[:, 48:64], lnw[:, 48:64], ALU.mult))
        ev_gb = vpinc()

        ln2_st = [None] * ND
        ev_xx_last = None
        for k in range(ND):
            if k >= 1:
                em.w("ve", ln2_st[k - 1])
            ev_xx = em.op("ve", lambda e, k=k: e.tensor_tensor(
                Fxx[:, :], X1V(k), X1V(k), ALU.mult), "v")
            ev_xx_last = ev_xx
            em.w("te", ev_xx)
            for evg in ffn_gate:
                em.w("te", evg)

            def st2(e, k=k):
                r = None
                for tc in range(8):
                    e.matmul(PS[2][:, k * 8 + tc:k * 8 + tc + 1],
                             X1V(k)[:, tc * 128:(tc + 1) * 128], onescol[:, :],
                             start=True, stop=True)
                for tc in range(8):
                    r = e.matmul(PS[2][:, 256 + k * 8 + tc:256 + k * 8 + tc + 1],
                                 Fxx[:, tc * 128:(tc + 1) * 128], onescol[:, :],
                                 start=True, stop=True)
                return r
            ln2_st[k] = em.op("te", st2, "t")

        em.w("ve", ln2_st[ND - 1])
        em.op("ve", lambda e: e.reduce_sum(
            ssum[:, 0:8], PS[2][:, 0:128].rearrange("p (a b) -> p b a", a=16), axis=AX))
        ev_red2 = em.op("ve", lambda e: e.reduce_sum(
            ssum[:, 16:24], PS[2][:, 256:384].rearrange("p (a b) -> p b a", a=16), axis=AX), "v")
        ev_bpq2 = emit_ln(8, ev_red2)
        if K_DBG:
            em.w("sy", ev_bpq2)
            ev_d9 = em.op("sy", lambda e: e.dma_start(
                out=dDBG[0:128, 1024:2048], in_=BP[:, 0:1024]), "x", 16)
            ev_d10 = em.op("sy", lambda e: e.dma_start(
                out=dDBG[0:128, 2048:3072], in_=BQ[:, 0:1024]), "x", 16)
            ev_d11 = em.op("sy", lambda e: e.dma_start(
                out=dDBG[0:128, 3072:3104], in_=ssum[:, :]), "x", 16)
            em.w("ve", ev_d11)

        h2_done = [None] * ND
        for k in range(ND):
            em.w("ve", ev_bpq2)
            em.op("ve", lambda e, k=k: e.tensor_tensor(
                Fu1[:, :], X1V(k), BP[:, 0:1024], ALU.mult))
            em.op("ve", lambda e: e.tensor_tensor(
                Fu2[:, :], Fu1[:, :], BQ[:, 0:1024], ALU.subtract))
            h2_done[k] = em.op("ve", lambda e, k=k: e.tensor_scalar(
                Hbf2V(k), Fu2[:, :], modx[:, 32 + k:33 + k], modv[:, 48 + k:49 + k],
                ALU.mult, ALU.add), "v")
        ev_h2_all = h2_done[ND - 1]
        if K_DBG:
            em.w("ve", ev_h2_all)
            ev_d1 = em.op("ve", lambda e: e.tensor_copy(Fu1[:, :], Hbf2V(0)), "v")
            em.w("sy", ev_d1)
            ev_d2 = em.op("sy", lambda e: e.dma_start(
                out=dDBG[0:128, 14336:15360], in_=Fu1[:, :]), "x", 16)
            em.w("ve", ev_d2)
            em.w("ve", ev_gb)
            ev_d3 = em.op("ve", lambda e: e.tensor_copy(Fu1[:, 0:96], modx[:, :]), "v")
            em.w("sy", ev_d3)
            ev_d4 = em.op("sy", lambda e: e.dma_start(
                out=dDBG[0:128, 640:736], in_=Fu1[:, 0:96]), "x", 16)
            em.w("ve", ev_d4)

        em.enabled = upto >= 5
        em.w("te", ev_h2_all)
        em.w("te", ev_gw)

        def rmm(e):
            r = None
            for k in range(ND):
                for s in range(2):
                    r = e.matmul(PS[2][0:16, s * 512:(s + 1) * 512],
                                 gw[:, k * 16:(k + 1) * 16],
                                 Hbf2V(k)[:, s * 512:(s + 1) * 512],
                                 start=(k == 0), stop=(k == ND - 1))
            return r
        ev_rmm = em.op("te", rmm, "t")
        em.w("se", ev_rmm)
        ev_tanh_r = em.op("se", lambda e: e.activation(
            scoresF[0:16, :], PS[2][0:16, 0:1024], AF.Tanh, scale=0.5), "s")
        em.w("ve", ev_tanh_r)
        ev_sig = em.op("ve", lambda e: e.tensor_scalar(
            scoresF[0:16, :], scoresF[0:16, :], 0.5, 0.5, ALU.mult, ALU.add), "v")
        em.w("te", ev_sig)
        em.w("te", ev_tail_all)

        def sctr(e):
            r = None
            for t in range(8):
                r = e.transpose(PS[3][:, 16 * t:16 * (t + 1)],
                                scoresF[0:16, 128 * t:128 * (t + 1)], ident[0:16, 0:16])
            return r
        ev_sctr = em.op("te", sctr, "t")
        em.w("se", ev_sctr)
        em.op("se", lambda e: e.copy(scoresTM[:, :], PS[3][:, 0:128]))
        ev_stm = spinc()

        em.w("ve", ev_stm)
        def TKt(tc):
            return TK[:, tc * 96:(tc + 1) * 96]

        for tc in range(8):
            for g in range(4):
                em.op("ve", lambda e, tc=tc, g=g: e.reduce_max(
                    TKt(tc)[:, g:g + 1],
                    scoresTM[:, 16 * tc + 4 * g:16 * tc + 4 * g + 4], axis=AX))
        for tc in range(8):
            em.op("ve", lambda e, tc=tc: e.reduce_max(
                TKR[:, tc * 8 + 1:tc * 8 + 2], TKt(tc)[:, 0:4], axis=AX))
        for tc in range(8):
            em.op("ve", lambda e, tc=tc: e.tensor_scalar(
                TKt(tc)[:, 8:12], TKt(tc)[:, 0:4], TKR[:, tc * 8 + 1:tc * 8 + 2],
                None, ALU.is_equal))
        for tc in range(8):
            em.op("ve", lambda e, tc=tc: e.scalar_tensor_tensor(
                TKt(tc)[:, 12:16], TKt(tc)[:, 8:12], -1e30, TKt(tc)[:, 0:4],
                ALU.mult, ALU.add))
        for tc in range(8):
            em.op("ve", lambda e, tc=tc: e.reduce_max(
                TKR[:, tc * 8 + 2:tc * 8 + 3], TKt(tc)[:, 12:16], axis=AX))
        for tc in range(8):
            em.op("ve", lambda e, tc=tc: e.tensor_scalar(
                TKt(tc)[:, 12:16], TKt(tc)[:, 0:4], TKR[:, tc * 8 + 2:tc * 8 + 3],
                None, ALU.is_ge))
        for tc in range(8):
            for g in range(4):
                em.op("ve", lambda e, tc=tc, g=g: e.tensor_scalar(
                    TKt(tc)[:, 16 + 4 * g:20 + 4 * g],
                    scoresTM[:, 16 * tc + 4 * g:16 * tc + 4 * g + 4],
                    TKt(tc)[:, 12 + g:13 + g], None, ALU.mult))
        for tc in range(8):
            em.op("ve", lambda e, tc=tc: e.tensor_copy(
                TKt(tc)[:, 32:48], TKt(tc)[:, 16:32]))
        for it in range(4):
            for tc in range(8):
                em.op("ve", lambda e, tc=tc: e.reduce_max(
                    TKR[:, tc * 8 + 3:tc * 8 + 4], TKt(tc)[:, 32:48], axis=AX))
            if it < 3:
                for tc in range(8):
                    em.op("ve", lambda e, tc=tc: e.tensor_scalar(
                        TKt(tc)[:, 64:80], TKt(tc)[:, 32:48],
                        TKR[:, tc * 8 + 3:tc * 8 + 4], None, ALU.is_equal))
                for tc in range(8):
                    em.op("ve", lambda e, tc=tc: e.scalar_tensor_tensor(
                        TKt(tc)[:, 32:48], TKt(tc)[:, 64:80], -1e30,
                        TKt(tc)[:, 32:48], ALU.mult, ALU.add))
        for tc in range(8):
            em.op("ve", lambda e, tc=tc: e.tensor_scalar(
                TKt(tc)[:, 80:96], TKt(tc)[:, 16:32],
                TKR[:, tc * 8 + 3:tc * 8 + 4], None, ALU.is_ge))
        for tc in range(8):
            em.op("ve", lambda e, tc=tc: e.tensor_tensor(
                TKt(tc)[:, 16:32], scoresTM[:, 16 * tc:16 * tc + 16],
                TKt(tc)[:, 80:96], ALU.mult))
        for tc in range(8):
            em.op("ve", lambda e, tc=tc: e.reduce_sum(
                lnw2[:, tc:tc + 1], TKt(tc)[:, 16:32], axis=AX))
        for tc in range(8):
            em.op("ve", lambda e, tc=tc: e.reciprocal(
                TKR[:, tc * 8:tc * 8 + 1], lnw2[:, tc:tc + 1]))
        for tc in range(8):
            em.op("ve", lambda e, tc=tc: e.tensor_scalar(
                combTM[:, 16 * tc:16 * tc + 16], TKt(tc)[:, 16:32],
                TKR[:, tc * 8:tc * 8 + 1], None, ALU.mult))
        ev_tk = vpinc()

        em.w("te", ev_tk)

        def cmtr(e):
            r = None
            for t in range(8):
                r = e.transpose(PS[3][0:16, 128 * t:128 * (t + 1)],
                                combTM[:, 16 * t:16 * (t + 1)], ident[:, :])
            return r
        ev_cmtr = em.op("te", cmtr, "t")
        em.w("se", ev_cmtr)
        ev_comb = em.op("se", lambda e: e.copy(combFM[0:16, :], PS[3][0:16, 0:1024]), "s")
        if K_DBG:
            em.w("sy", ev_comb)
            em.w("sy", ev_tk)
            ev_d7 = em.op("sy", lambda e: e.dma_start(
                out=dDBG[0:16, 15360:16384], in_=combFM[0:16, :]), "x", 16)
            ev_d8 = em.op("sy", lambda e: e.dma_start(
                out=dDBG[0:128, 736:864], in_=scoresTM[:, :]), "x", 16)
            em.w("gp", ev_d7)
            em.w("gp", ev_d8)

        em.enabled = upto >= 6
        ws1_load = [None] * NSH
        ws3_load = [None] * NSH
        ws_used = [None] * NSH
        ws2_load = [None] * ND
        ws2_used = [None] * ND
        z_evac = [None] * NSH
        z_done = [None] * ND

        def emit_ws13_dma(sc):
            for evg in (ev_tk, ev_comb, ev_h2_all):
                em.w("gp", evg)
            if sc >= 2:
                em.w("gp", ws_used[sc - 2])
            ws1_load[sc] = em.op("gp", lambda e, sc=sc: e.dma_start(
                out=WS1S[sc % 2][:, :].rearrange("p (a b) -> p a b", a=16),
                in_=dWs1[sc]), "d", 16)
            ws3_load[sc] = em.op("gp", lambda e, sc=sc: e.dma_start(
                out=WS3S[sc % 2][:, :].rearrange("p (a b) -> p a b", a=16),
                in_=dWs3[sc]), "d", 16)

        def emit_ws2_dma(dc):
            for evg in (ev_h2_all, ev_tk, ev_cmtr):
                em.w("gp", evg)
            if dc >= 2:
                em.w("gp", ws2_used[dc - 2])
            ws2_load[dc] = em.op("gp", lambda e, dc=dc: e.dma_start(
                out=WS2S[dc % 2][:, :].rearrange("p (a b) -> p a b", a=22),
                in_=dWs2[dc]), "d", 16)

        emit_ws13_dma(0)
        emit_ws13_dma(1)
        for sc in range(NSH):
            psa = PS[0] if sc % 2 == 0 else PS[2]
            psb = PS[1] if sc % 2 == 0 else PS[3]
            em.w("te", ws1_load[sc])
            em.w("te", ws3_load[sc])
            if sc >= 2:
                em.w("te", z_evac[sc - 2])
            elif sc == 0:
                em.w("te", ev_bpq2)
            else:
                em.w("te", ev_tanh_r)
                em.w("te", ev_comb)

            def smm(e, sc=sc, psa=psa, psb=psb):
                r = None
                for s in range(2):
                    for dc2 in range(ND):
                        e.matmul(psa[:, s * 512:(s + 1) * 512],
                                 WS1S[sc % 2][:, dc2 * 128:(dc2 + 1) * 128],
                                 Hbf2V(dc2)[:, s * 512:(s + 1) * 512],
                                 start=(dc2 == 0), stop=(dc2 == ND - 1))
                for s in range(2):
                    for dc2 in range(ND):
                        r = e.matmul(psb[:, s * 512:(s + 1) * 512],
                                     WS3S[sc % 2][:, dc2 * 128:(dc2 + 1) * 128],
                                     Hbf2V(dc2)[:, s * 512:(s + 1) * 512],
                                     start=(dc2 == 0), stop=(dc2 == ND - 1))
                return r
            evm = em.op("te", smm, "t")
            ws_used[sc] = evm
            em.w("se", evm)
            if sc >= 1:
                em.w("se", z_evac[sc - 1])
            evs = em.op("se", lambda e, psa=psa: e.activation(
                ZTsil[:, :], psa[:, :], AF.Silu), "s")
            em.w("ve", evs)
            z_evac[sc] = em.op("ve", lambda e, sc=sc, psb=psb: e.tensor_tensor(
                ZbfV(sc), ZTsil[:, :], psb[:, :], ALU.mult), "v")
            if sc + 2 < NSH:
                emit_ws13_dma(sc + 2)
            if sc == NSH - 2:
                emit_ws2_dma(0)
                emit_ws2_dma(1)

        if K_DBG:
            em.w("ve", z_evac[NSH - 1])
            ev_d5 = em.op("ve", lambda e: e.tensor_copy(ZTsil[:, :], ZbfV(0)), "v")
            em.w("sy", ev_d5)
            ev_d6 = em.op("sy", lambda e: e.dma_start(
                out=dDBG[0:128, 10240:11264], in_=ZTsil[:, :]), "x", 16)
            em.w("ve", ev_d6)
            em.w("se", ev_d6)
        ev_z_mm_last = None
        for dc in range(ND):
            ps = PS[0] if dc % 2 == 0 else PS[1]
            em.w("te", z_evac[NSH - 1])
            em.w("te", ws2_load[dc])
            if dc >= 2:
                em.w("te", z_done[dc - 2])

            def zmm(e, dc=dc, ps=ps):
                r = None
                for s in range(2):
                    for sc in range(NSH):
                        r = e.matmul(ps[:, s * 512:(s + 1) * 512],
                                     WS2S[dc % 2][:, sc * 128:(sc + 1) * 128],
                                     ZbfV(sc)[:, s * 512:(s + 1) * 512],
                                     start=(sc == 0), stop=(sc == NSH - 1))
                return r
            evm = em.op("te", zmm, "t")
            ws2_used[dc] = evm
            ev_z_mm_last = evm
            em.w("ve", evm)
            em.w("ve", ev_gb)
            z_done[dc] = em.op("ve", lambda e, dc=dc, ps=ps: e.scalar_tensor_tensor(
                X1V(dc), ps[:, :], modx[:, 48 + dc:49 + dc], X1V(dc),
                ALU.mult, ALU.add), "v")
            if dc + 2 < ND:
                emit_ws2_dma(dc + 2)

        em.enabled = upto >= 7
        w1_load = {}
        w3_load = {}
        w13_used = {}
        w2_load = {}
        w2_used = {}
        g_done = {}
        sil_done = {}
        y_done = list(z_done)

        def emit_w13_dma(ee, ic):
            idx = ee * NI + ic
            em.w("gp", ev_z_mm_last)
            em.w("gp", z_evac[NSH - 1])
            if idx >= 2:
                pe, pic = divmod(idx - 2, NI)
                em.w("gp", w13_used[(pe, pic)])
            w1_load[(ee, ic)] = em.op("gp", lambda e, ee=ee, ic=ic: e.dma_start(
                out=W13S[ic % 2][:, :].rearrange("p (a b) -> p a b", a=16),
                in_=dW1[ee, ic]), "d", 16)
            w3_load[(ee, ic)] = em.op("gp", lambda e, ee=ee, ic=ic: e.dma_start(
                out=W13S[2 + ic % 2][:, :].rearrange("p (a b) -> p a b", a=16),
                in_=dW3[ee, ic]), "d", 16)

        def emit_w2_dma(ee, dc):
            j = ee * ND + dc
            em.w("gp", ev_z_mm_last)
            if j >= 2:
                pe, pdc = divmod(j - 2, ND)
                em.w("gp", w2_used[(pe, pdc)])
            w2_load[(ee, dc)] = em.op("gp", lambda e, ee=ee, dc=dc: e.dma_start(
                out=W2S[dc % 2][:, :].rearrange("p (a b) -> p a b", a=NI),
                in_=dW2[ee, dc]), "d", 16)

        emit_w13_dma(0, 0)
        emit_w13_dma(0, 1)
        for ee in range(NE):
            em.w("te", ev_comb)
            if ee >= 1:
                em.w("te", sil_done[(ee - 1, NI - 2)])
            else:
                em.w("te", ev_tanh_r)

            def cbm(e, ee=ee):
                r = None
                for s in range(2):
                    r = e.matmul(PS[2][:, s * 512:(s + 1) * 512],
                                 sels[:, ee * 128:(ee + 1) * 128],
                                 combFM[0:16, s * 512:(s + 1) * 512],
                                 start=True, stop=True)
                return r
            ev_cb = em.op("te", cbm, "t")
            em.w("se", ev_cb)
            ev_cbc = em.op("se", lambda e, ee=ee: e.copy(
                CBC[ee % 2][:, :], PS[2][:, :]), "s")

            for ic in range(NI):
                psa = PS[0] if ic % 2 == 0 else PS[2]
                psb = PS[1] if ic % 2 == 0 else PS[3]
                em.w("te", w1_load[(ee, ic)])
                em.w("te", w3_load[(ee, ic)])
                if ic >= 2:
                    em.w("te", g_done[(ee, ic - 2)])
                elif ic == 0:
                    em.w("te", y_done[ND - 2])
                    em.w("te", y_done[ND - 1])
                else:
                    em.w("te", ev_cbc)
                    if ee >= 1:
                        em.w("te", g_done[(ee - 1, NI - 2)])
                    else:
                        em.w("te", z_evac[NSH - 1])

                def hmm_(e, ee=ee, ic=ic, psa=psa, psb=psb):
                    r = None
                    for s in range(2):
                        for dc2 in range(ND):
                            e.matmul(psa[:, s * 512:(s + 1) * 512],
                                     W13S[ic % 2][:, dc2 * 128:(dc2 + 1) * 128],
                                     Hbf2V(dc2)[:, s * 512:(s + 1) * 512],
                                     start=(dc2 == 0), stop=(dc2 == ND - 1))
                    for s in range(2):
                        for dc2 in range(ND):
                            r = e.matmul(psb[:, s * 512:(s + 1) * 512],
                                         W13S[2 + ic % 2][:, dc2 * 128:(dc2 + 1) * 128],
                                         Hbf2V(dc2)[:, s * 512:(s + 1) * 512],
                                         start=(dc2 == 0), stop=(dc2 == ND - 1))
                    return r
                evm = em.op("te", hmm_, "t")
                w13_used[(ee, ic)] = evm
                em.w("se", evm)
                if ic >= 1:
                    em.w("se", g_done[(ee, ic - 1)])
                elif ee >= 1:
                    em.w("se", g_done[(ee - 1, NI - 1)])
                evs = em.op("se", lambda e, psa=psa: e.activation(
                    Tsil[:, :], psa[:, :], AF.Silu), "s")
                sil_done[(ee, ic)] = evs
                em.w("ve", evs)
                em.w("ve", ev_cbc)
                em.op("ve", lambda e, psb=psb: e.tensor_tensor(
                    Tgf[:, :], Tsil[:, :], psb[:, :], ALU.mult))
                g_done[(ee, ic)] = em.op("ve", lambda e, ee=ee, ic=ic: e.tensor_tensor(
                    GbfV(ic), Tgf[:, :], CBC[ee % 2][:, :], ALU.mult), "v")
                nidx = ee * NI + ic + 2
                if nidx < NE * NI:
                    emit_w13_dma(*divmod(nidx, NI))
                if ic == NI - 1 and ee == 0:
                    emit_w2_dma(0, 0)
                    emit_w2_dma(0, 1)

            for dc in range(ND):
                ps = PS[0] if dc % 2 == 0 else PS[1]
                em.w("te", g_done[(ee, NI - 1)])
                em.w("te", w2_load[(ee, dc)])
                if dc >= 2:
                    em.w("te", y_done[dc - 2])

                def ymm(e, ee=ee, dc=dc, ps=ps):
                    r = None
                    for s in range(2):
                        for ic in range(NI):
                            r = e.matmul(ps[:, s * 512:(s + 1) * 512],
                                         W2S[dc % 2][:, ic * 128:(ic + 1) * 128],
                                         GbfV(ic)[:, s * 512:(s + 1) * 512],
                                         start=(ic == 0), stop=(ic == NI - 1))
                    return r
                evm = em.op("te", ymm, "t")
                w2_used[(ee, dc)] = evm
                em.w("ve", evm)
                y_done[dc] = em.op("ve", lambda e, dc=dc, ps=ps: e.scalar_tensor_tensor(
                    X1V(dc), ps[:, :], modx[:, 48 + dc:49 + dc], X1V(dc),
                    ALU.mult, ALU.add), "v")
                nj = ee * ND + dc + 2
                if nj < NE * ND:
                    emit_w2_dma(*divmod(nj, ND))

        em.enabled = True
        for s in Em.SEMS:
            em.ops["sy"].append(("w", s, em.cnt[s]))
            em.hw["sy"][s] = max(em.hw["sy"][s], em.cnt[s])
        for k in range(ND):
            em.w("sy", y_done[k])
            em.op("sy", lambda e, k=k: e.dma_start(
                out=dOUT[k * 128:(k + 1) * 128, :], in_=X1V(k)), "x", 16)
        em.ops["sy"].append(("w", "x", em.cnt["x"]))
        em.hw["sy"]["x"] = em.cnt["x"]

        def replay(eng_obj, name):
            for item in em.ops[name]:
                if item[0] == "w":
                    eng_obj.wait_ge(sem[item[1]], item[2])
                else:
                    _, fn, inc, amt = item
                    instr = fn(eng_obj)
                    if inc is not None:
                        assert instr is not None, "group fn must return last instr"
                        instr.then_inc(sem[inc], amt)

        @block.sync
        def _(sync):
            replay(sync, "sy")

        @block.gpsimd
        def _(gpsimd):
            replay(gpsimd, "gp")

        @block.tensor
        def _(tensor):
            replay(tensor, "te")

        @block.vector
        def _(vector):
            replay(vector, "ve")

        @block.scalar
        def _(scalar):
            replay(scalar, "se")

    return nc


def _prep_weights(i):
    win = np.ascontiguousarray(
        np.asarray(i["L0_attn_W_in"]).reshape(32, 128, 16, 128).transpose(0, 3, 2, 1))
    wmod = np.ascontiguousarray(
        np.asarray(i["L0_wmod"]).reshape(16, 128, 96, 128).transpose(2, 1, 0, 3))
    w1 = np.ascontiguousarray(
        np.asarray(i["L0_ffn_W1"]).reshape(16, 11, 128, 16, 128).transpose(0, 1, 4, 3, 2))
    w3 = np.ascontiguousarray(
        np.asarray(i["L0_ffn_W3"]).reshape(16, 11, 128, 16, 128).transpose(0, 1, 4, 3, 2))
    w2 = np.ascontiguousarray(
        np.asarray(i["L0_ffn_W2"]).reshape(16, 16, 128, 11, 128).transpose(0, 1, 4, 3, 2))
    ws1 = np.ascontiguousarray(
        np.asarray(i["L0_ffn_Ws1"]).reshape(22, 128, 16, 128).transpose(0, 3, 2, 1))
    ws3 = np.ascontiguousarray(
        np.asarray(i["L0_ffn_Ws3"]).reshape(22, 128, 16, 128).transpose(0, 3, 2, 1))
    ws2 = np.ascontiguousarray(
        np.asarray(i["L0_ffn_Ws2"]).reshape(16, 128, 22, 128).transpose(0, 3, 2, 1))
    gwh = np.ascontiguousarray(
        np.asarray(i["L0_ffn_gate_w"]).reshape(16, 16, 128).transpose(2, 1, 0))
    return dict(win=win, wmod=wmod, w1=w1, w3=w3, w2=w2, ws1=ws1, ws3=ws3,
                ws2=ws2, gw=gwh)


def kernel(**inputs):
    x = np.asarray(inputs["L0_attn_x"])
    cfull = np.asarray(inputs["L0_c"])
    apar = np.asarray(inputs["L0_attn_a_param"])

    shared = _prep_weights(inputs)
    ident = np.eye(128, dtype=np.float32)
    selsh = np.zeros((16, 2048), dtype=np.float32)
    for c in range(16):
        selsh[c, 128 * c:128 * (c + 1)] = 1.0
    apcol = np.ascontiguousarray(apar.reshape(128, 1).astype(np.float32))

    in_maps = []
    for core in range(8):
        b, half = core // 2, core % 2
        full = np.zeros((2048, 2048), dtype=np.float32)
        if half == 1:
            full[0:1024] = x[b, 0:1024]
        full[1024:2048] = x[b, half * 1024:(half + 1) * 1024]
        xf = np.ascontiguousarray(full.T)
        cvec = np.ascontiguousarray(cfull[b].reshape(16, 128).T)
        mc = np.full((128, 1), 1.0 if half == 1 else 0.0, dtype=np.float32)
        m = dict(shared)
        m.update(xf=xf, cvec=cvec, mcol=mc, apcol=apcol, ident=ident, sels=selsh)
        in_maps.append(m)

    nc = build_nc()
    res = run_bass_kernel_spmd(nc, in_maps, core_ids=list(range(8)))

    out = np.empty((4, 2048, 2048), dtype=np.float32)
    for core in range(8):
        b, half = core // 2, core % 2
        out[b, half * 1024:(half + 1) * 1024, :] = res.results[core]["out"].T
    return out


# version 05 (final)
# speedup vs baseline: 1.0087x; 1.0087x over previous
import os
import numpy as np
import concourse.bass as bass
import concourse.mybir as mybir
from concourse.bass_utils import run_bass_kernel_spmd

F32 = mybir.dt.float32
BF16 = mybir.dt.bfloat16
AF = mybir.ActivationFunctionType
ALU = mybir.AluOpType
AX = mybir.AxisListType.X

D = 2048
ND = 16
SLOTS = 2048
NQ = 4
NE = 16
NI = 11
NSH = 22
NJC = 96
EPS = 1e-5
POOL_N = 28672
PB = 20480


class Em:
    ENG = ("sy", "gp", "te", "ve", "se")
    SEMS = ("d", "x", "t", "v", "s")

    def __init__(self):
        self.ops = {e: [] for e in self.ENG}
        self.cnt = {s: 0 for s in self.SEMS}
        self.hw = {e: {s: 0 for s in self.SEMS} for e in self.ENG}
        self.enabled = True

    def w(self, eng, ev):
        if not self.enabled:
            return
        if ev is None:
            return
        s, val = ev
        if self.hw[eng][s] >= val:
            return
        self.hw[eng][s] = val
        self.ops[eng].append(("w", s, val))

    def op(self, eng, fn, inc=None, amt=1):
        if not self.enabled:
            return None
        ev = None
        if inc is not None:
            self.cnt[inc] += amt
            ev = (inc, self.cnt[inc])
        self.ops[eng].append(("o", fn, inc, amt))
        return ev


def build_nc(upto=7):
    nc = bass.Bass("TRN2", target_bir_lowering=False)

    dXF = nc.dram_tensor("xf", [D, SLOTS], F32, kind="ExternalInput").ap()
    dWin = nc.dram_tensor("win", [32, 128, 16, 128], F32, kind="ExternalInput").ap()
    dWmod = nc.dram_tensor("wmod", [NJC, 128, 16, 128], F32, kind="ExternalInput").ap()
    dW1 = nc.dram_tensor("w1", [NE, NI, 128, 16, 128], F32, kind="ExternalInput").ap()
    dW3 = nc.dram_tensor("w3", [NE, NI, 128, 16, 128], F32, kind="ExternalInput").ap()
    dW2 = nc.dram_tensor("w2", [NE, ND, 128, NI, 128], F32, kind="ExternalInput").ap()
    dWs1 = nc.dram_tensor("ws1", [NSH, 128, 16, 128], F32, kind="ExternalInput").ap()
    dWs3 = nc.dram_tensor("ws3", [NSH, 128, 16, 128], F32, kind="ExternalInput").ap()
    dWs2 = nc.dram_tensor("ws2", [ND, 128, NSH, 128], F32, kind="ExternalInput").ap()
    dGW = nc.dram_tensor("gw", [128, 16, 16], F32, kind="ExternalInput").ap()
    dC = nc.dram_tensor("cvec", [128, 16], F32, kind="ExternalInput").ap()
    dAP = nc.dram_tensor("apcol", [128, 1], F32, kind="ExternalInput").ap()
    dMC = nc.dram_tensor("mcol", [128, 1], F32, kind="ExternalInput").ap()
    dID = nc.dram_tensor("ident", [128, 128], F32, kind="ExternalInput").ap()
    dSEL = nc.dram_tensor("sels", [16, 2048], F32, kind="ExternalInput").ap()
    dOUT = nc.dram_tensor("out", [D, 1024], F32, kind="ExternalOutput").ap()
    K_DBG = os.environ.get("K_DBG") == "1"
    dDBG = nc.dram_tensor("dbg", [128, 16384], F32, kind="ExternalOutput").ap() if K_DBG else None

    em = Em()
    from contextlib import ExitStack
    with ExitStack() as ctx:
        E = ctx.enter_context
        pool = E(nc.sbuf_tensor("pool", [128, POOL_N], F32))
        X1 = E(nc.sbuf_tensor("x1", [128, ND * 1024], F32))
        BP = E(nc.sbuf_tensor("bp", [128, 2048], F32))
        BQ = E(nc.sbuf_tensor("bq", [128, 2048], F32))
        ident = E(nc.sbuf_tensor("identsb", [128, 128], F32))
        sels = E(nc.sbuf_tensor("selssb", [16, 2048], F32))
        modv = E(nc.sbuf_tensor("modv", [128, 96], F32))
        modx = E(nc.sbuf_tensor("modx", [128, 96], F32))
        lnw = E(nc.sbuf_tensor("lnw", [128, 64], F32))
        lnw2 = E(nc.sbuf_tensor("lnw2", [128, 32], F32))
        ssum = E(nc.sbuf_tensor("ssum", [128, 32], F32))
        TKR2 = E(nc.sbuf_tensor("tkr2", [128, 8], F32))
        PTs = E(nc.sbuf_tensor("pts", [16, 128], F32))
        QTs = E(nc.sbuf_tensor("qts", [16, 128], F32))
        onescol = E(nc.sbuf_tensor("onescol", [128, 1], F32))
        padv = E(nc.sbuf_tensor("padv", [128, 448], F32))
        pads = E(nc.sbuf_tensor("pads", [128, 384], F32))
        epscol = E(nc.sbuf_tensor("epscol", [128, 1], F32))
        stcol = E(nc.sbuf_tensor("stcol", [128, 1], F32))
        mcol = E(nc.sbuf_tensor("mcolsb", [128, 1], F32))
        apm4 = E(nc.sbuf_tensor("apm4", [128, 1], F32))
        apsp = E(nc.sbuf_tensor("apsp", [128, 1], F32))
        apraw = E(nc.sbuf_tensor("apraw", [128, 1], F32))
        cvec = E(nc.sbuf_tensor("cvecsb", [128, 16], F32))
        siluc = E(nc.sbuf_tensor("siluc", [128, 16], F32))
        scolb = E(nc.sbuf_tensor("scolb", [128, 16], BF16))
        gw = E(nc.sbuf_tensor("gwb", [128, 256], BF16))
        PS = [E(nc.psum_tensor(f"ps{i}", [128, 1024], F32)) for i in range(4)]
        sem = {s: E(nc.semaphore(f"sem_{s}")) for s in Em.SEMS}
        block = E(nc.Block())

        def pv(off, n, dt=F32):
            apx = pool[:, off:off + n]
            return apx if dt is F32 else apx.bitcast(dt)

        Hbf = pv(0, 16384, BF16)
        WINS = [pv(16384 + 1024 * i, 1024, BF16) for i in range(3)]
        WMS = [pv(19456, 1024, BF16), pv(PB + 6144, 1024, BF16)]
        xb = [pv(PB, 2048), pv(PB + 2048, 2048)]
        Txx = pv(PB + 4096, 2048)
        U1f = pv(PB + 4096, 2048)
        U2f = pv(PB + 6144, 2048)
        TxQ = [pv(PB, 512), pv(PB + 512, 512)]
        U1q = pv(PB + 1024, 512)
        U2q = pv(PB + 1536, 512)
        Tgx = [pv(PB + 2048, 512), pv(PB + 2560, 512)]
        Tla = [pv(PB + 3072, 512), pv(PB + 3584, 512)]
        Ta = [pv(PB + 4096, 512), pv(PB + 4608, 512)]
        Tmu = [pv(PB + 5120, 512), pv(PB + 5632, 512)]
        Hbf2 = pv(0, 8192, BF16)
        Gbf = pv(8192, 5632, BF16)
        W13S = [pv(13824 + 1024 * i, 1024, BF16) for i in range(4)]
        W2S = [pv(17920, 704, BF16), pv(18624, 704, BF16)]
        CBC = [pv(19328, 1024), pv(20352, 1024)]
        Tsil = pv(21376, 1024)
        Tgf = pv(22400, 1024)
        Fu1 = pv(23424, 1024)
        Fu2 = pv(24448, 1024)
        Fxx = pv(25472, 1024)
        scoresTM = pv(25472, 128)
        combTM = pv(25600, 128)
        TK = pv(25728, 768)
        TKR = pv(27520, 64)
        scoresF = pv(26496, 1024)
        combFM = pv(27648, 1024)
        Zbf = pv(8192, 11264, BF16)
        WS1S = [pv(19456, 1024, BF16), pv(20480, 1024, BF16)]
        WS3S = [pv(21504, 1024, BF16), pv(22528, 1024, BF16)]
        WS2S = [pv(23552, 1408, BF16), pv(24960, 1408, BF16)]
        ZTsil = pv(26368, 1024)

        def HbfV(k):
            return Hbf[:, k * 2048:(k + 1) * 2048]

        def Hbf2V(k):
            return Hbf2[:, k * 1024:(k + 1) * 1024]

        def X1V(k):
            return X1[:, k * 1024:(k + 1) * 1024]

        def GbfV(ic):
            return Gbf[:, ic * 1024:(ic + 1) * 1024]

        def ZbfV(sc):
            return Zbf[:, sc * 1024:(sc + 1) * 1024]

        def vpad():
            em.op("ve", lambda e: e.tensor_copy(padv[:, :], X1[:, 0:448]))

        def vpinc():
            return em.op("ve", lambda e: e.tensor_copy(padv[:, :], X1[:, 0:448]), "v")

        def spinc():
            return em.op("se", lambda e: e.copy(pads[:, :], X1[:, 0:384]), "s")

        ev_c = em.op("sy", lambda e: e.dma_start(out=cvec[:, :], in_=dC[:, :]), "x", 16)
        ev_ap = em.op("sy", lambda e: e.dma_start(out=apraw[:, :], in_=dAP[:, :]), "x", 16)
        ev_mc = em.op("sy", lambda e: e.dma_start(out=mcol[:, :], in_=dMC[:, :]), "x", 16)
        ev_id = em.op("sy", lambda e: e.dma_start(out=ident[:, :], in_=dID[:, :]), "x", 16)
        ev_sel = em.op("sy", lambda e: e.dma_start(out=sels[:, :], in_=dSEL[:, :]), "x", 16)
        ev_gw = em.op("gp", lambda e: e.dma_start(
            out=gw[:, :].rearrange("p (a b) -> p a b", a=16), in_=dGW[:, :, :]), "d", 16)

        em.op("ve", lambda e: e.memset(onescol[:, :], 1.0))
        em.op("ve", lambda e: e.memset(epscol[:, :], EPS))
        ev_ones = vpinc()
        ev_eps = ev_ones
        em.w("se", ev_c)
        em.op("se", lambda e: e.activation(siluc[:, :], cvec[:, :], AF.Silu))
        ev_sc = spinc()
        em.w("ve", ev_sc)
        em.op("ve", lambda e: e.tensor_copy(scolb[:, :], siluc[:, :]))
        ev_scb = vpinc()
        em.w("se", ev_ap)
        em.op("se", lambda e: e.activation(apsp[:, :], apraw[:, :], AF.Exp))
        ev_ex = spinc()
        em.w("ve", ev_ex)
        em.op("ve", lambda e: e.tensor_scalar(
            apraw[:, :], apsp[:, :], 1.0, None, ALU.add))
        ev_p1 = vpinc()
        em.w("se", ev_p1)
        em.op("se", lambda e: e.activation(apsp[:, :], apraw[:, :], AF.Ln))
        ev_sp = spinc()
        em.w("ve", ev_sp)
        em.op("ve", lambda e: e.tensor_scalar(
            apm4[:, :], apsp[:, :], -4.0, None, ALU.mult))
        ev_ap4 = vpinc()

        wm_load = [None] * NJC
        wm_mm = [None] * NJC

        def emit_wmod_dma(jc, extra_waits=()):
            for wv in extra_waits:
                em.w("gp", wv)
            if jc >= 2:
                em.w("gp", wm_mm[jc - 2])
            wm_load[jc] = em.op(
                "gp", lambda e, jc=jc: e.dma_start(
                    out=WMS[jc % 2][:, :].rearrange("p (a b) -> p a b", a=16),
                    in_=dWmod[jc]), "d", 16)

        def emit_wmod_mm(jc, extra_waits=()):
            em.w("te", wm_load[jc])
            em.w("te", ev_scb)
            for wv in extra_waits:
                em.w("te", wv)

            def mmfn(e, jc=jc):
                r = None
                for dc in range(16):
                    r = e.matmul(PS[3][:, jc:jc + 1],
                                 WMS[jc % 2][:, dc * 128:(dc + 1) * 128],
                                 scolb[:, dc:dc + 1],
                                 start=(dc == 0), stop=(dc == 15))
                return r
            wm_mm[jc] = em.op("te", mmfn, "t")

        K_WMOD = os.environ.get("K_WMOD") == "1"
        K_STATS = os.environ.get("K_STATS") == "1"
        xa_load = [None] * ND
        xa_sq = [None] * ND
        xa_st = [None] * ND
        for k in range(ND):
            if not K_WMOD:
                for j in (2 * k, 2 * k + 1):
                    emit_wmod_dma(j)
            if k >= 2:
                em.w("sy", xa_st[k - 2])
                em.w("sy", xa_sq[k - 2])
            if not K_STATS:
                xa_load[k] = em.op(
                    "sy", lambda e, k=k: e.dma_start(
                        out=xb[k % 2][:, :], in_=dXF[k * 128:(k + 1) * 128, :]), "x", 16)
                em.w("ve", xa_load[k])
                if k >= 1:
                    em.w("ve", xa_st[k - 1])
                xa_sq[k] = em.op(
                    "ve", lambda e, k=k: e.tensor_tensor(
                        Txx[:, :], xb[k % 2][:, :], xb[k % 2][:, :], ALU.mult), "v")
            if not K_WMOD:
                for j in (2 * k, 2 * k + 1):
                    emit_wmod_mm(j)
            if K_STATS:
                continue
            em.w("te", xa_load[k])
            em.w("te", xa_sq[k])
            em.w("te", ev_ones)

            def stfn(e, k=k):
                r = None
                for tc in range(16):
                    e.matmul(PS[2][:, k * 16 + tc:k * 16 + tc + 1],
                             xb[k % 2][:, tc * 128:(tc + 1) * 128], onescol[:, :],
                             start=True, stop=True)
                for tc in range(16):
                    r = e.matmul(PS[2][:, 256 + k * 16 + tc:256 + k * 16 + tc + 1],
                                 Txx[:, tc * 128:(tc + 1) * 128], onescol[:, :],
                                 start=True, stop=True)
                return r
            xa_st[k] = em.op("te", stfn, "t")

        if K_WMOD:
            wm_mm = [None] * NJC
        em.w("ve", wm_mm[31])
        em.op("ve", lambda e: e.tensor_copy(modv[:, 0:32], PS[3][:, 0:32]))
        ev_pmh = vpinc()
        em.w("ve", ev_mc)
        em.op("ve", lambda e: e.tensor_scalar(
            modx[:, 0:16], modv[:, 16:32], 1.0, 1.0, ALU.mult, ALU.add))
        vpad()
        em.op("ve", lambda e: e.tensor_scalar(
            modx[:, 64:80], modx[:, 0:16], mcol[:, :], None, ALU.mult))
        em.op("ve", lambda e: e.tensor_scalar(
            modx[:, 80:96], modv[:, 0:16], mcol[:, :], None, ALU.mult))
        ev_modh = vpinc()

        def emit_ln(ntc, ev_stats):
            n = float(D)
            em.w("ve", ev_stats)
            em.op("ve", lambda e: e.tensor_scalar(
                lnw[:, 0:ntc], ssum[:, 0:ntc], 1.0 / n, None, ALU.mult))
            em.op("ve", lambda e: e.tensor_scalar(
                lnw[:, 16:16 + ntc], ssum[:, 16:16 + ntc], 1.0 / n, None, ALU.mult))
            vpad()
            em.op("ve", lambda e: e.tensor_tensor(
                lnw[:, 32:32 + ntc], lnw[:, 0:ntc], lnw[:, 0:ntc], ALU.mult))
            vpad()
            em.op("ve", lambda e: e.tensor_tensor(
                lnw[:, 16:16 + ntc], lnw[:, 16:16 + ntc], lnw[:, 32:32 + ntc],
                ALU.subtract))
            ev_var = vpinc()
            if K_DBG and ntc == 16:
                em.w("sy", ev_var)
                ev_dv = em.op("sy", lambda e: e.dma_start(
                    out=dDBG[0:128, 576:640], in_=lnw[:, :]), "x", 16)
                em.w("se", ev_dv)
            em.w("se", ev_var)
            em.w("se", ev_eps)
            em.op("se", lambda e: e.activation(
                lnw[:, 32:32 + ntc], lnw[:, 16:16 + ntc], AF.Sqrt, bias=epscol[:, :]))
            ev_std = spinc()
            em.w("ve", ev_std)
            em.op("ve", lambda e: e.reciprocal(lnw2[:, 0:ntc], lnw[:, 32:32 + ntc]))
            vpad()
            em.op("ve", lambda e: e.tensor_tensor(
                lnw2[:, 16:16 + ntc], lnw[:, 0:ntc], lnw2[:, 0:ntc], ALU.mult))
            ev_pq = vpinc()
            em.w("te", ev_pq)
            em.w("te", ev_id)
            em.op("te", lambda e: e.transpose(
                PS[1][0:ntc, 0:128], lnw2[:, 0:ntc], ident[:, :]), "t")
            ev_tq = em.op("te", lambda e: e.transpose(
                PS[1][0:ntc, 128:256], lnw2[:, 16:16 + ntc], ident[:, :]), "t")
            em.w("se", ev_tq)
            em.op("se", lambda e: e.copy(PTs[0:ntc, :], PS[1][0:ntc, 0:128]))
            em.op("se", lambda e: e.copy(QTs[0:ntc, :], PS[1][0:ntc, 128:256]))
            ev_eq = spinc()
            if K_DBG and ntc == 8:
                em.w("sy", ev_eq)
                em.op("sy", lambda e: e.dma_start(
                    out=dDBG[0:128, 3136:3200], in_=lnw[:, :]), "x", 16)
                em.op("sy", lambda e: e.dma_start(
                    out=dDBG[0:128, 3200:3232], in_=lnw2[:, :]), "x", 16)
                em.op("sy", lambda e: e.dma_start(
                    out=dDBG[0:16, 3232:3360], in_=PTs[0:16, :]), "x", 16)
                ev_dd = em.op("sy", lambda e: e.dma_start(
                    out=dDBG[0:16, 3360:3488], in_=QTs[0:16, :]), "x", 16)
                em.w("te", ev_dd)
                em.w("ve", ev_dd)
            em.w("te", ev_eq)
            em.w("te", ev_sel)
            done = None
            gdone = [None] * (ntc // 4)
            for g in range(ntc // 4):
                pst = PS[0] if g % 2 == 0 else PS[1]
                if g >= 2:
                    em.w("te", gdone[g - 2])

                def bcf(e, g=g, pst=pst):
                    r = None
                    for i in range(4):
                        c = 4 * g + i
                        e.matmul(pst[:, i * 128:(i + 1) * 128],
                                 sels[:, c * 128:(c + 1) * 128], PTs[0:16, :],
                                 start=True, stop=True)
                    for i in range(4):
                        c = 4 * g + i
                        r = e.matmul(pst[:, 512 + i * 128:512 + (i + 1) * 128],
                                     sels[:, c * 128:(c + 1) * 128], QTs[0:16, :],
                                     start=True, stop=True)
                    return r
                evb = em.op("te", bcf, "t")
                em.w("ve", evb)
                em.op("ve", lambda e, g=g, pst=pst: e.tensor_copy(
                    BP[:, g * 512:(g + 1) * 512], pst[:, 0:512]))
                done = em.op("ve", lambda e, g=g, pst=pst: e.tensor_copy(
                    BQ[:, g * 512:(g + 1) * 512], pst[:, 512:1024]), "v")
                gdone[g] = done
            return done

        em.w("ve", xa_st[ND - 1])
        if not K_STATS:
            em.op("ve", lambda e: e.reduce_sum(
                ssum[:, 0:16], PS[2][:, 0:256].rearrange("p (a b) -> p b a", a=16), axis=AX))
        ev_red1 = em.op("ve", lambda e: (e.reduce_sum(
            ssum[:, 16:32], PS[2][:, 256:512].rearrange("p (a b) -> p b a", a=16), axis=AX)
            if not K_STATS else e.memset(ssum[:, :], 1.0)), "v")
        em.enabled = upto >= 2
        ev_bpq1 = emit_ln(16, ev_red1)
        if K_DBG:
            em.w("sy", ev_bpq1)
            em.w("sy", ev_modh)
            em.op("sy", lambda e: e.dma_start(out=dDBG[0:128, 0:96], in_=modv[:, :]), "x", 16)
            em.op("sy", lambda e: e.dma_start(out=dDBG[0:128, 96:192], in_=modx[:, :]), "x", 16)
            em.op("sy", lambda e: e.dma_start(out=dDBG[0:128, 192:224], in_=ssum[:, :]), "x", 16)
            em.op("sy", lambda e: e.dma_start(out=dDBG[0:128, 224:288], in_=lnw[:, :]), "x", 16)
            em.op("sy", lambda e: e.dma_start(out=dDBG[0:128, 288:320], in_=lnw2[:, :]), "x", 16)
            em.op("sy", lambda e: e.dma_start(out=dDBG[0:16, 320:448], in_=PTs[:, :]), "x", 16)
            em.op("sy", lambda e: e.dma_start(out=dDBG[0:16, 448:576], in_=QTs[:, :]), "x", 16)
            em.op("sy", lambda e: e.dma_start(out=dDBG[0:128, 1024:3072], in_=BP[:, :]), "x", 16)
            em.op("sy", lambda e: e.dma_start(out=dDBG[0:128, 3072:5120], in_=BQ[:, :]), "x", 16)

        xb_load = [None] * ND
        hb_done = [None] * ND
        for k in range(ND):
            if k >= 2:
                em.w("sy", hb_done[k - 2])
            else:
                em.w("sy", xa_st[ND - 2 + k])
                em.w("sy", xa_sq[ND - 2 + k])
            xb_load[k] = em.op(
                "sy", lambda e, k=k: e.dma_start(
                    out=xb[k % 2][:, :], in_=dXF[k * 128:(k + 1) * 128, :]), "x", 16)
            em.w("ve", xb_load[k])
            em.w("ve", ev_bpq1)
            em.w("ve", ev_modh)
            em.op("ve", lambda e, k=k: e.tensor_tensor(
                U1f[:, :], xb[k % 2][:, :], BP[:, :], ALU.mult))
            em.op("ve", lambda e: e.tensor_tensor(
                U2f[:, :], U1f[:, :], BQ[:, :], ALU.subtract))
            hb_done[k] = em.op("ve", lambda e, k=k: e.tensor_scalar(
                HbfV(k), U2f[:, :], modx[:, 0 + k:1 + k], modv[:, 0 + k:1 + k],
                ALU.mult, ALU.add), "v")
        ev_hbf_all = hb_done[ND - 1]
        ev_passB_dead = hb_done[ND - 1]
        if K_DBG:
            em.w("ve", ev_hbf_all)
            ev_hd = em.op("ve", lambda e: e.tensor_copy(U1f[:, :], HbfV(0)), "v")
            em.w("sy", ev_hd)
            ev_hd2 = em.op("sy", lambda e: e.dma_start(
                out=dDBG[0:128, 5120:7168], in_=U1f[:, :]), "x", 16)
            em.w("ve", ev_hd2)

        em.enabled = upto >= 3
        win_load = [None] * 32
        win_used = [None] * 32
        ev_unit_tanh = [None] * 64
        ev_unit_mm = [None] * 64
        scan_ev = [None] * 64
        txq_use = [None] * 64
        ev_tail_evac = [None] * ND
        se_last = None

        for k in range(ND):
            tail_waits = [ev_pmh] if k == 0 else [ev_tail_evac[k - 1]]
            for i in range(2):
                jc = 32 + 4 * k + i
                extra_d = [ev_passB_dead] if (jc % 2 == 1) else []
                emit_wmod_dma(jc, extra_waits=extra_d)
            for g in range(2):
                i = 2 * k + g
                if i >= 3:
                    em.w("gp", win_used[i - 3])
                oc = g * 16 + k
                win_load[i] = em.op(
                    "gp", lambda e, i=i, oc=oc: e.dma_start(
                        out=WINS[i % 3][:, :].rearrange("p (a b) -> p a b", a=16),
                        in_=dWin[oc]), "d", 16)
            for i in range(2):
                jc = 32 + 4 * k + i
                emit_wmod_mm(jc, extra_waits=(tail_waits if i == 0 else []))
            for i in range(2, 4):
                jc = 32 + 4 * k + i
                extra_d = [ev_passB_dead] if (jc % 2 == 1) else []
                emit_wmod_dma(jc, extra_waits=extra_d)
            for i in range(2, 4):
                jc = 32 + 4 * k + i
                emit_wmod_mm(jc)
            em.w("ve", wm_mm[32 + 4 * k + 3])
            em.op("ve", lambda e, k=k: e.tensor_copy(
                modv[:, 32 + 4 * k:36 + 4 * k], PS[3][:, 32 + 4 * k:36 + 4 * k]))
            ev_tail_evac[k] = vpinc()
            if k < 4:
                c0 = 32 + 4 * k
                o0 = 16 + 4 * k
                em.op("ve", lambda e, c0=c0: e.tensor_scalar(
                    lnw[:, 56:60], modv[:, c0:c0 + 4], -1.0, None, ALU.mult))
                em.op("ve", lambda e, c0=c0: e.tensor_scalar(
                    lnw[:, 48:52], modv[:, c0:c0 + 4], 0.0, None, ALU.is_ge))
                vpad()
                em.op("ve", lambda e, c0=c0, o0=o0: e.tensor_tensor(
                    modx[:, o0:o0 + 4], modv[:, c0:c0 + 4], lnw[:, 56:60], ALU.max))
                em.op("ve", lambda e: e.tensor_scalar(
                    lnw[:, 52:56], lnw[:, 48:52], 2.0, -1.0, ALU.mult, ALU.add))
                vpad()
                em.op("ve", lambda e, o0=o0: e.tensor_scalar(
                    modx[:, o0:o0 + 4], modx[:, o0:o0 + 4], 0.25, None, ALU.max))
                vpad()
                em.op("ve", lambda e, o0=o0: e.tensor_tensor(
                    modx[:, o0:o0 + 4], modx[:, o0:o0 + 4], lnw[:, 52:56], ALU.mult))
                vpad()

            for q in range(NQ):
                u = 4 * k + q
                ps = PS[u % 3]
                st = q % 2
                em.w("te", ev_hbf_all)
                em.w("te", ev_bpq1)
                em.w("te", win_load[2 * k])
                em.w("te", win_load[2 * k + 1])
                if u >= 3:
                    em.w("te", ev_unit_tanh[u - 3])

                def gmm(e, k=k, q=q, ps=ps):
                    r = None
                    for kk in range(ND):
                        e.matmul(ps[:, 0:512],
                                 WINS[(2 * k) % 3][:, kk * 128:(kk + 1) * 128],
                                 HbfV(kk)[:, q * 512:(q + 1) * 512],
                                 start=(kk == 0), stop=(kk == ND - 1))
                    for kk in range(ND):
                        r = e.matmul(ps[:, 512:1024],
                                     WINS[(2 * k + 1) % 3][:, kk * 128:(kk + 1) * 128],
                                     HbfV(kk)[:, q * 512:(q + 1) * 512],
                                     start=(kk == 0), stop=(kk == ND - 1))
                    return r
                ev_unit_mm[u] = em.op("te", gmm, "t")
                if q == NQ - 1:
                    win_used[2 * k] = ev_unit_mm[u]
                    win_used[2 * k + 1] = ev_unit_mm[u]
                em.w("se", ev_unit_mm[u])
                if u >= 2:
                    em.w("se", scan_ev[u - 2])
                em.op("se", lambda e, ps=ps, st=st: e.activation(
                    Tgx[st][:, :], ps[:, 0:512], AF.Tanh, scale=0.5))
                ev_unit_tanh[u] = em.op("se", lambda e, ps=ps, st=st: e.activation(
                    Tla[st][:, :], ps[:, 512:1024], AF.Tanh, scale=0.5), "s")
                em.w("ve", ev_unit_tanh[u])
                em.w("ve", ev_ap4)
                ev_la = em.op("ve", lambda e, st=st: e.tensor_scalar(
                    Tla[st][:, :], Tla[st][:, :], apm4[:, :], apm4[:, :],
                    ALU.mult, ALU.add), "v")
                em.op("ve", lambda e, st=st: e.tensor_scalar(
                    Tgx[st][:, :], Tgx[st][:, :], 0.5, 0.5, ALU.mult, ALU.add))
                em.w("se", ev_la)
                em.op("se", lambda e, st=st: e.activation(
                    Ta[st][:, :], Tla[st][:, :], AF.Exp))
                ev_e2 = em.op("se", lambda e, st=st: e.activation(
                    Tmu[st][:, :], Ta[st][:, :], AF.Square), "s")
                em.w("ve", ev_e2)
                em.op("ve", lambda e, st=st: e.tensor_scalar(
                    Tmu[st][:, :], Tmu[st][:, :], -1.0, 1.0, ALU.mult, ALU.add))
                ev_om = em.op("ve", lambda e, st=st: e.tensor_scalar(
                    Tmu[st][:, :], Tmu[st][:, :], 0.0, None, ALU.max), "v")
                em.w("se", ev_om)
                ev_mu = em.op("se", lambda e, st=st: e.activation(
                    Tla[st][:, :], Tmu[st][:, :], AF.Sqrt), "s")
                se_last = ev_mu
                if u >= 2:
                    em.w("sy", txq_use[u - 2])
                else:
                    em.w("sy", ev_hbf_all)
                ev_xq = em.op("sy", lambda e, k=k, q=q, u=u: e.dma_start(
                    out=TxQ[u % 2][:, :],
                    in_=dXF[k * 128:(k + 1) * 128, q * 512:(q + 1) * 512]), "x", 16)
                em.w("ve", ev_xq)
                em.op("ve", lambda e, u=u, q=q: e.tensor_tensor(
                    U1q[:, :], TxQ[u % 2][:, :], BP[:, q * 512:(q + 1) * 512], ALU.mult))
                em.op("ve", lambda e, q=q: e.tensor_tensor(
                    U2q[:, :], U1q[:, :], BQ[:, q * 512:(q + 1) * 512], ALU.subtract))
                if q < 2:
                    ca_ap, sa_ap = modx[:, 64 + k:65 + k], modx[:, 80 + k:81 + k]
                else:
                    ca_ap, sa_ap = modx[:, 0 + k:1 + k], modv[:, 0 + k:1 + k]
                em.op("ve", lambda e, ca_ap=ca_ap, sa_ap=sa_ap: e.tensor_scalar(
                    U1q[:, :], U2q[:, :], ca_ap, sa_ap, ALU.mult, ALU.add))
                em.op("ve", lambda e, st=st: e.tensor_tensor(
                    U2q[:, :], U1q[:, :], Tgx[st][:, :], ALU.mult))
                em.w("ve", ev_mu)
                em.op("ve", lambda e, st=st: e.tensor_tensor(
                    U1q[:, :], U2q[:, :], Tla[st][:, :], ALU.mult))
                init = 0.0 if q == 0 else stcol[:, :]
                if os.environ.get("K_SCAN") == "1":
                    scan_ev[u] = em.op("ve", lambda e, st=st: e.tensor_copy(
                        U2q[:, :], U1q[:, :]), "v")
                else:
                    scan_ev[u] = em.op("ve", lambda e, st=st, init=init: e.tensor_tensor_scan(
                        U2q[:, :], Ta[st][:, :], U1q[:, :], init, ALU.mult, ALU.add), "v")
                if K_DBG and u in (0, 2):
                    base = 7168 if u == 0 else 11264
                    em.w("sy", scan_ev[u])
                    for i2, src_ap in enumerate(
                            [Tgx[st], Tla[st], Ta[st], U1q, U2q, TxQ[u % 2]]):
                        ev_dd = em.op("sy", lambda e, i2=i2, base=base, src_ap=src_ap:
                                      e.dma_start(out=dDBG[0:128, base + i2 * 512:
                                                           base + (i2 + 1) * 512],
                                                  in_=src_ap[:, :]), "x", 16)
                    em.w("ve", ev_dd)
                    em.w("se", ev_dd)
                em.op("ve", lambda e: e.tensor_copy(stcol[:, :], U2q[:, 511:512]))
                if q >= 2:
                    txq_use[u] = em.op("ve", lambda e, k=k, q=q, u=u: e.scalar_tensor_tensor(
                        X1V(k)[:, (q - 2) * 512:(q - 1) * 512],
                        U2q[:, :], modx[:, 16 + k:17 + k], TxQ[u % 2][:, :],
                        ALU.mult, ALU.add), "v")
                else:
                    txq_use[u] = scan_ev[u]

        ev_passC_ve = txq_use[63]
        ev_passC_te = ev_unit_mm[63]
        ev_passC_se = se_last
        ev_tail_all = ev_tail_evac[ND - 1]
        ffn_gate = [ev_passC_ve, ev_passC_te, ev_passC_se, ev_tail_all]

        em.enabled = upto >= 4
        for evg in ffn_gate:
            em.w("ve", evg)
        em.op("ve", lambda e: e.tensor_scalar(
            modx[:, 32:48], modv[:, 64:80], 1.0, 1.0, ALU.mult, ALU.add))
        em.op("ve", lambda e: e.tensor_scalar(
            lnw[:, 0:16], modv[:, 80:96], -1.0, None, ALU.mult))
        em.op("ve", lambda e: e.tensor_scalar(
            lnw[:, 48:64], modv[:, 80:96], 0.0, None, ALU.is_ge))
        vpad()
        em.op("ve", lambda e: e.tensor_tensor(
            modx[:, 48:64], modv[:, 80:96], lnw[:, 0:16], ALU.max))
        em.op("ve", lambda e: e.tensor_scalar(
            lnw[:, 48:64], lnw[:, 48:64], 2.0, -1.0, ALU.mult, ALU.add))
        vpad()
        em.op("ve", lambda e: e.tensor_scalar(
            modx[:, 48:64], modx[:, 48:64], 0.25, None, ALU.max))
        vpad()
        em.op("ve", lambda e: e.tensor_tensor(
            modx[:, 48:64], modx[:, 48:64], lnw[:, 48:64], ALU.mult))
        ev_gb = vpinc()

        ln2_st = [None] * ND
        ev_xx_last = None
        for k in range(ND):
            if k >= 1:
                em.w("ve", ln2_st[k - 1])
            ev_xx = em.op("ve", lambda e, k=k: e.tensor_tensor(
                Fxx[:, :], X1V(k), X1V(k), ALU.mult), "v")
            ev_xx_last = ev_xx
            em.w("te", ev_xx)
            for evg in ffn_gate:
                em.w("te", evg)

            def st2(e, k=k):
                r = None
                for tc in range(8):
                    e.matmul(PS[2][:, k * 8 + tc:k * 8 + tc + 1],
                             X1V(k)[:, tc * 128:(tc + 1) * 128], onescol[:, :],
                             start=True, stop=True)
                for tc in range(8):
                    r = e.matmul(PS[2][:, 256 + k * 8 + tc:256 + k * 8 + tc + 1],
                                 Fxx[:, tc * 128:(tc + 1) * 128], onescol[:, :],
                                 start=True, stop=True)
                return r
            ln2_st[k] = em.op("te", st2, "t")

        em.w("ve", ln2_st[ND - 1])
        em.op("ve", lambda e: e.reduce_sum(
            ssum[:, 0:8], PS[2][:, 0:128].rearrange("p (a b) -> p b a", a=16), axis=AX))
        ev_red2 = em.op("ve", lambda e: e.reduce_sum(
            ssum[:, 16:24], PS[2][:, 256:384].rearrange("p (a b) -> p b a", a=16), axis=AX), "v")
        ev_bpq2 = emit_ln(8, ev_red2)
        if K_DBG:
            em.w("sy", ev_bpq2)
            ev_d9 = em.op("sy", lambda e: e.dma_start(
                out=dDBG[0:128, 1024:2048], in_=BP[:, 0:1024]), "x", 16)
            ev_d10 = em.op("sy", lambda e: e.dma_start(
                out=dDBG[0:128, 2048:3072], in_=BQ[:, 0:1024]), "x", 16)
            ev_d11 = em.op("sy", lambda e: e.dma_start(
                out=dDBG[0:128, 3072:3104], in_=ssum[:, :]), "x", 16)
            em.w("ve", ev_d11)

        h2_done = [None] * ND
        for k in range(ND):
            em.w("ve", ev_bpq2)
            em.op("ve", lambda e, k=k: e.tensor_tensor(
                Fu1[:, :], X1V(k), BP[:, 0:1024], ALU.mult))
            em.op("ve", lambda e: e.tensor_tensor(
                Fu2[:, :], Fu1[:, :], BQ[:, 0:1024], ALU.subtract))
            h2_done[k] = em.op("ve", lambda e, k=k: e.tensor_scalar(
                Hbf2V(k), Fu2[:, :], modx[:, 32 + k:33 + k], modv[:, 48 + k:49 + k],
                ALU.mult, ALU.add), "v")
        ev_h2_all = h2_done[ND - 1]
        if K_DBG:
            em.w("ve", ev_h2_all)
            ev_d1 = em.op("ve", lambda e: e.tensor_copy(Fu1[:, :], Hbf2V(0)), "v")
            em.w("sy", ev_d1)
            ev_d2 = em.op("sy", lambda e: e.dma_start(
                out=dDBG[0:128, 14336:15360], in_=Fu1[:, :]), "x", 16)
            em.w("ve", ev_d2)
            em.w("ve", ev_gb)
            ev_d3 = em.op("ve", lambda e: e.tensor_copy(Fu1[:, 0:96], modx[:, :]), "v")
            em.w("sy", ev_d3)
            ev_d4 = em.op("sy", lambda e: e.dma_start(
                out=dDBG[0:128, 640:736], in_=Fu1[:, 0:96]), "x", 16)
            em.w("ve", ev_d4)

        em.enabled = upto >= 5
        em.w("te", ev_h2_all)
        em.w("te", ev_gw)

        def rmm(e):
            r = None
            for k in range(ND):
                for s in range(2):
                    r = e.matmul(PS[2][0:16, s * 512:(s + 1) * 512],
                                 gw[:, k * 16:(k + 1) * 16],
                                 Hbf2V(k)[:, s * 512:(s + 1) * 512],
                                 start=(k == 0), stop=(k == ND - 1))
            return r
        ev_rmm = em.op("te", rmm, "t")
        em.w("se", ev_rmm)
        ev_tanh_r = em.op("se", lambda e: e.activation(
            scoresF[0:16, :], PS[2][0:16, 0:1024], AF.Tanh, scale=0.5), "s")
        em.w("ve", ev_tanh_r)
        ev_sig = em.op("ve", lambda e: e.tensor_scalar(
            scoresF[0:16, :], scoresF[0:16, :], 0.5, 0.5, ALU.mult, ALU.add), "v")
        em.w("te", ev_sig)
        em.w("te", ev_tail_all)

        def sctr(e):
            r = None
            for t in range(8):
                r = e.transpose(PS[3][:, 16 * t:16 * (t + 1)],
                                scoresF[0:16, 128 * t:128 * (t + 1)], ident[0:16, 0:16])
            return r
        ev_sctr = em.op("te", sctr, "t")
        em.w("se", ev_sctr)
        em.op("se", lambda e: e.copy(scoresTM[:, :], PS[3][:, 0:128]))
        ev_stm = spinc()

        em.w("ve", ev_stm)
        def TKt(tc):
            return TK[:, tc * 96:(tc + 1) * 96]

        for tc in range(8):
            for g in range(4):
                em.op("ve", lambda e, tc=tc, g=g: e.reduce_max(
                    TKt(tc)[:, g:g + 1],
                    scoresTM[:, 16 * tc + 4 * g:16 * tc + 4 * g + 4], axis=AX))
        for tc in range(8):
            em.op("ve", lambda e, tc=tc: e.reduce_max(
                TKR[:, tc * 8 + 1:tc * 8 + 2], TKt(tc)[:, 0:4], axis=AX))
        for tc in range(8):
            em.op("ve", lambda e, tc=tc: e.tensor_scalar(
                TKt(tc)[:, 8:12], TKt(tc)[:, 0:4], TKR[:, tc * 8 + 1:tc * 8 + 2],
                None, ALU.is_equal))
        for tc in range(8):
            em.op("ve", lambda e, tc=tc: e.scalar_tensor_tensor(
                TKt(tc)[:, 12:16], TKt(tc)[:, 8:12], -1e30, TKt(tc)[:, 0:4],
                ALU.mult, ALU.add))
        for tc in range(8):
            em.op("ve", lambda e, tc=tc: e.reduce_max(
                TKR[:, tc * 8 + 2:tc * 8 + 3], TKt(tc)[:, 12:16], axis=AX))
        for tc in range(8):
            em.op("ve", lambda e, tc=tc: e.tensor_scalar(
                TKt(tc)[:, 12:16], TKt(tc)[:, 0:4], TKR[:, tc * 8 + 2:tc * 8 + 3],
                None, ALU.is_ge))
        for tc in range(8):
            for g in range(4):
                em.op("ve", lambda e, tc=tc, g=g: e.tensor_scalar(
                    TKt(tc)[:, 16 + 4 * g:20 + 4 * g],
                    scoresTM[:, 16 * tc + 4 * g:16 * tc + 4 * g + 4],
                    TKt(tc)[:, 12 + g:13 + g], None, ALU.mult))
        for tc in range(8):
            em.op("ve", lambda e, tc=tc: e.tensor_copy(
                TKt(tc)[:, 32:48], TKt(tc)[:, 16:32]))
        for it in range(4):
            for tc in range(8):
                em.op("ve", lambda e, tc=tc: e.reduce_max(
                    TKR[:, tc * 8 + 3:tc * 8 + 4], TKt(tc)[:, 32:48], axis=AX))
            if it < 3:
                for tc in range(8):
                    em.op("ve", lambda e, tc=tc: e.tensor_scalar(
                        TKt(tc)[:, 64:80], TKt(tc)[:, 32:48],
                        TKR[:, tc * 8 + 3:tc * 8 + 4], None, ALU.is_equal))
                for tc in range(8):
                    em.op("ve", lambda e, tc=tc: e.scalar_tensor_tensor(
                        TKt(tc)[:, 32:48], TKt(tc)[:, 64:80], -1e30,
                        TKt(tc)[:, 32:48], ALU.mult, ALU.add))
        for tc in range(8):
            em.op("ve", lambda e, tc=tc: e.tensor_scalar(
                TKt(tc)[:, 80:96], TKt(tc)[:, 16:32],
                TKR[:, tc * 8 + 3:tc * 8 + 4], None, ALU.is_ge))
        for tc in range(8):
            em.op("ve", lambda e, tc=tc: e.tensor_tensor(
                TKt(tc)[:, 16:32], scoresTM[:, 16 * tc:16 * tc + 16],
                TKt(tc)[:, 80:96], ALU.mult))
        for tc in range(8):
            em.op("ve", lambda e, tc=tc: e.reduce_sum(
                lnw2[:, tc:tc + 1], TKt(tc)[:, 16:32], axis=AX))
        for tc in range(8):
            em.op("ve", lambda e, tc=tc: e.reciprocal(
                TKR[:, tc * 8:tc * 8 + 1], lnw2[:, tc:tc + 1]))
        for tc in range(8):
            em.op("ve", lambda e, tc=tc: e.tensor_scalar(
                combTM[:, 16 * tc:16 * tc + 16], TKt(tc)[:, 16:32],
                TKR[:, tc * 8:tc * 8 + 1], None, ALU.mult))
        ev_tk = vpinc()

        em.w("te", ev_tk)

        def cmtr(e):
            r = None
            for t in range(8):
                r = e.transpose(PS[3][0:16, 128 * t:128 * (t + 1)],
                                combTM[:, 16 * t:16 * (t + 1)], ident[:, :])
            return r
        ev_cmtr = em.op("te", cmtr, "t")
        em.w("se", ev_cmtr)
        ev_comb = em.op("se", lambda e: e.copy(combFM[0:16, :], PS[3][0:16, 0:1024]), "s")
        if K_DBG:
            em.w("sy", ev_comb)
            em.w("sy", ev_tk)
            ev_d7 = em.op("sy", lambda e: e.dma_start(
                out=dDBG[0:16, 15360:16384], in_=combFM[0:16, :]), "x", 16)
            ev_d8 = em.op("sy", lambda e: e.dma_start(
                out=dDBG[0:128, 736:864], in_=scoresTM[:, :]), "x", 16)
            em.w("gp", ev_d7)
            em.w("gp", ev_d8)

        em.enabled = upto >= 6
        ws1_load = [None] * NSH
        ws3_load = [None] * NSH
        ws_used = [None] * NSH
        ws2_load = [None] * ND
        ws2_used = [None] * ND
        z_evac = [None] * NSH
        z_done = [None] * ND

        def emit_ws13_dma(sc):
            for evg in (ev_tk, ev_comb, ev_h2_all):
                em.w("gp", evg)
            if sc >= 2:
                em.w("gp", ws_used[sc - 2])
            ws1_load[sc] = em.op("gp", lambda e, sc=sc: e.dma_start(
                out=WS1S[sc % 2][:, :].rearrange("p (a b) -> p a b", a=16),
                in_=dWs1[sc]), "d", 16)
            ws3_load[sc] = em.op("gp", lambda e, sc=sc: e.dma_start(
                out=WS3S[sc % 2][:, :].rearrange("p (a b) -> p a b", a=16),
                in_=dWs3[sc]), "d", 16)

        def emit_ws2_dma(dc):
            for evg in (ev_h2_all, ev_tk, ev_cmtr):
                em.w("gp", evg)
            if dc >= 2:
                em.w("gp", ws2_used[dc - 2])
            ws2_load[dc] = em.op("gp", lambda e, dc=dc: e.dma_start(
                out=WS2S[dc % 2][:, :].rearrange("p (a b) -> p a b", a=22),
                in_=dWs2[dc]), "d", 16)

        emit_ws13_dma(0)
        emit_ws13_dma(1)
        for sc in range(NSH):
            psa = PS[0] if sc % 2 == 0 else PS[2]
            psb = PS[1] if sc % 2 == 0 else PS[3]
            em.w("te", ws1_load[sc])
            em.w("te", ws3_load[sc])
            if sc >= 2:
                em.w("te", z_evac[sc - 2])
            elif sc == 0:
                em.w("te", ev_bpq2)
            else:
                em.w("te", ev_tanh_r)
                em.w("te", ev_comb)

            def smm(e, sc=sc, psa=psa, psb=psb):
                r = None
                for s in range(2):
                    for dc2 in range(ND):
                        e.matmul(psa[:, s * 512:(s + 1) * 512],
                                 WS1S[sc % 2][:, dc2 * 128:(dc2 + 1) * 128],
                                 Hbf2V(dc2)[:, s * 512:(s + 1) * 512],
                                 start=(dc2 == 0), stop=(dc2 == ND - 1))
                for s in range(2):
                    for dc2 in range(ND):
                        r = e.matmul(psb[:, s * 512:(s + 1) * 512],
                                     WS3S[sc % 2][:, dc2 * 128:(dc2 + 1) * 128],
                                     Hbf2V(dc2)[:, s * 512:(s + 1) * 512],
                                     start=(dc2 == 0), stop=(dc2 == ND - 1))
                return r
            evm = em.op("te", smm, "t")
            ws_used[sc] = evm
            em.w("se", evm)
            if sc >= 1:
                em.w("se", z_evac[sc - 1])
            evs = em.op("se", lambda e, psa=psa: e.activation(
                ZTsil[:, :], psa[:, :], AF.Silu), "s")
            em.w("ve", evs)
            z_evac[sc] = em.op("ve", lambda e, sc=sc, psb=psb: e.tensor_tensor(
                ZbfV(sc), ZTsil[:, :], psb[:, :], ALU.mult), "v")
            if sc + 2 < NSH:
                emit_ws13_dma(sc + 2)
            if sc == NSH - 2:
                emit_ws2_dma(0)
                emit_ws2_dma(1)

        if K_DBG:
            em.w("ve", z_evac[NSH - 1])
            ev_d5 = em.op("ve", lambda e: e.tensor_copy(ZTsil[:, :], ZbfV(0)), "v")
            em.w("sy", ev_d5)
            ev_d6 = em.op("sy", lambda e: e.dma_start(
                out=dDBG[0:128, 10240:11264], in_=ZTsil[:, :]), "x", 16)
            em.w("ve", ev_d6)
            em.w("se", ev_d6)
        ev_z_mm_last = None
        for dc in range(ND):
            ps = PS[0] if dc % 2 == 0 else PS[1]
            em.w("te", z_evac[NSH - 1])
            em.w("te", ws2_load[dc])
            if dc >= 2:
                em.w("te", z_done[dc - 2])

            def zmm(e, dc=dc, ps=ps):
                r = None
                for s in range(2):
                    for sc in range(NSH):
                        r = e.matmul(ps[:, s * 512:(s + 1) * 512],
                                     WS2S[dc % 2][:, sc * 128:(sc + 1) * 128],
                                     ZbfV(sc)[:, s * 512:(s + 1) * 512],
                                     start=(sc == 0), stop=(sc == NSH - 1))
                return r
            evm = em.op("te", zmm, "t")
            ws2_used[dc] = evm
            ev_z_mm_last = evm
            em.w("ve", evm)
            em.w("ve", ev_gb)
            z_done[dc] = em.op("ve", lambda e, dc=dc, ps=ps: e.scalar_tensor_tensor(
                X1V(dc), ps[:, :], modx[:, 48 + dc:49 + dc], X1V(dc),
                ALU.mult, ALU.add), "v")
            if dc + 2 < ND:
                emit_ws2_dma(dc + 2)

        em.enabled = upto >= 7
        w1_load = {}
        w3_load = {}
        w13_used = {}
        w2_load = {}
        w2_used = {}
        g_done = {}
        sil_done = {}
        y_done = list(z_done)

        def emit_w13_dma(ee, ic):
            idx = ee * NI + ic
            em.w("gp", ev_z_mm_last)
            em.w("gp", z_evac[NSH - 1])
            if idx >= 2:
                pe, pic = divmod(idx - 2, NI)
                em.w("gp", w13_used[(pe, pic)])
            w1_load[(ee, ic)] = em.op("gp", lambda e, ee=ee, ic=ic: e.dma_start(
                out=W13S[ic % 2][:, :].rearrange("p (a b) -> p a b", a=16),
                in_=dW1[ee, ic]), "d", 16)
            w3_load[(ee, ic)] = em.op("gp", lambda e, ee=ee, ic=ic: e.dma_start(
                out=W13S[2 + ic % 2][:, :].rearrange("p (a b) -> p a b", a=16),
                in_=dW3[ee, ic]), "d", 16)

        def emit_w2_dma(ee, dc):
            j = ee * ND + dc
            em.w("gp", ev_z_mm_last)
            if j >= 2:
                pe, pdc = divmod(j - 2, ND)
                em.w("gp", w2_used[(pe, pdc)])
            w2_load[(ee, dc)] = em.op("gp", lambda e, ee=ee, dc=dc: e.dma_start(
                out=W2S[dc % 2][:, :].rearrange("p (a b) -> p a b", a=NI),
                in_=dW2[ee, dc]), "d", 16)

        emit_w13_dma(0, 0)
        emit_w13_dma(0, 1)
        for ee in range(NE):
            em.w("te", ev_comb)
            if ee >= 1:
                em.w("te", sil_done[(ee - 1, NI - 2)])
            else:
                em.w("te", ev_tanh_r)

            def cbm(e, ee=ee):
                r = None
                for s in range(2):
                    r = e.matmul(PS[2][:, s * 512:(s + 1) * 512],
                                 sels[:, ee * 128:(ee + 1) * 128],
                                 combFM[0:16, s * 512:(s + 1) * 512],
                                 start=True, stop=True)
                return r
            ev_cb = em.op("te", cbm, "t")
            em.w("se", ev_cb)
            ev_cbc = em.op("se", lambda e, ee=ee: e.copy(
                CBC[ee % 2][:, :], PS[2][:, :]), "s")

            for ic in range(NI):
                psa = PS[0] if ic % 2 == 0 else PS[2]
                psb = PS[1] if ic % 2 == 0 else PS[3]
                em.w("te", w1_load[(ee, ic)])
                em.w("te", w3_load[(ee, ic)])
                if ic >= 2:
                    em.w("te", g_done[(ee, ic - 2)])
                elif ic == 0:
                    em.w("te", y_done[ND - 2])
                    em.w("te", y_done[ND - 1])
                else:
                    em.w("te", ev_cbc)
                    if ee >= 1:
                        em.w("te", g_done[(ee - 1, NI - 2)])
                    else:
                        em.w("te", z_evac[NSH - 1])

                def hmm_(e, ee=ee, ic=ic, psa=psa, psb=psb):
                    r = None
                    for s in range(2):
                        for dc2 in range(ND):
                            e.matmul(psa[:, s * 512:(s + 1) * 512],
                                     W13S[ic % 2][:, dc2 * 128:(dc2 + 1) * 128],
                                     Hbf2V(dc2)[:, s * 512:(s + 1) * 512],
                                     start=(dc2 == 0), stop=(dc2 == ND - 1))
                    for s in range(2):
                        for dc2 in range(ND):
                            r = e.matmul(psb[:, s * 512:(s + 1) * 512],
                                         W13S[2 + ic % 2][:, dc2 * 128:(dc2 + 1) * 128],
                                         Hbf2V(dc2)[:, s * 512:(s + 1) * 512],
                                         start=(dc2 == 0), stop=(dc2 == ND - 1))
                    return r
                evm = em.op("te", hmm_, "t")
                w13_used[(ee, ic)] = evm
                em.w("se", evm)
                if ic >= 1:
                    em.w("se", g_done[(ee, ic - 1)])
                elif ee >= 1:
                    em.w("se", g_done[(ee - 1, NI - 1)])
                evs = em.op("se", lambda e, psa=psa: e.activation(
                    Tsil[:, :], psa[:, :], AF.Silu), "s")
                sil_done[(ee, ic)] = evs
                em.w("ve", evs)
                em.w("ve", ev_cbc)
                em.op("ve", lambda e, psb=psb: e.tensor_tensor(
                    Tgf[:, :], Tsil[:, :], psb[:, :], ALU.mult))
                g_done[(ee, ic)] = em.op("ve", lambda e, ee=ee, ic=ic: e.tensor_tensor(
                    GbfV(ic), Tgf[:, :], CBC[ee % 2][:, :], ALU.mult), "v")
                nidx = ee * NI + ic + 2
                if nidx < NE * NI:
                    emit_w13_dma(*divmod(nidx, NI))
                if ic == NI - 1 and ee == 0:
                    emit_w2_dma(0, 0)
                    emit_w2_dma(0, 1)

            for dc in range(ND):
                ps = PS[0] if dc % 2 == 0 else PS[1]
                em.w("te", g_done[(ee, NI - 1)])
                em.w("te", w2_load[(ee, dc)])
                if dc >= 2:
                    em.w("te", y_done[dc - 2])

                def ymm(e, ee=ee, dc=dc, ps=ps):
                    r = None
                    for s in range(2):
                        for ic in range(NI):
                            r = e.matmul(ps[:, s * 512:(s + 1) * 512],
                                         W2S[dc % 2][:, ic * 128:(ic + 1) * 128],
                                         GbfV(ic)[:, s * 512:(s + 1) * 512],
                                         start=(ic == 0), stop=(ic == NI - 1))
                    return r
                evm = em.op("te", ymm, "t")
                w2_used[(ee, dc)] = evm
                em.w("ve", evm)
                y_done[dc] = em.op("ve", lambda e, dc=dc, ps=ps: e.scalar_tensor_tensor(
                    X1V(dc), ps[:, :], modx[:, 48 + dc:49 + dc], X1V(dc),
                    ALU.mult, ALU.add), "v")
                nj = ee * ND + dc + 2
                if nj < NE * ND:
                    emit_w2_dma(*divmod(nj, ND))

        em.enabled = True
        for s in Em.SEMS:
            em.ops["sy"].append(("w", s, em.cnt[s]))
            em.hw["sy"][s] = max(em.hw["sy"][s], em.cnt[s])
        for k in range(ND):
            em.w("sy", y_done[k])
            em.op("sy", lambda e, k=k: e.dma_start(
                out=dOUT[k * 128:(k + 1) * 128, :], in_=X1V(k)), "x", 16)
        em.ops["sy"].append(("w", "x", em.cnt["x"]))
        em.hw["sy"]["x"] = em.cnt["x"]

        def replay(eng_obj, name):
            for item in em.ops[name]:
                if item[0] == "w":
                    eng_obj.wait_ge(sem[item[1]], item[2])
                else:
                    _, fn, inc, amt = item
                    instr = fn(eng_obj)
                    if inc is not None:
                        assert instr is not None, "group fn must return last instr"
                        instr.then_inc(sem[inc], amt)

        @block.sync
        def _(sync):
            replay(sync, "sy")

        @block.gpsimd
        def _(gpsimd):
            replay(gpsimd, "gp")

        @block.tensor
        def _(tensor):
            replay(tensor, "te")

        @block.vector
        def _(vector):
            replay(vector, "ve")

        @block.scalar
        def _(scalar):
            replay(scalar, "se")

    return nc


def _prep_weights(i):
    win = np.ascontiguousarray(
        np.asarray(i["L0_attn_W_in"]).reshape(32, 128, 16, 128).transpose(0, 3, 2, 1))
    wmod = np.ascontiguousarray(
        np.asarray(i["L0_wmod"]).reshape(16, 128, 96, 128).transpose(2, 1, 0, 3))
    w1 = np.ascontiguousarray(
        np.asarray(i["L0_ffn_W1"]).reshape(16, 11, 128, 16, 128).transpose(0, 1, 4, 3, 2))
    w3 = np.ascontiguousarray(
        np.asarray(i["L0_ffn_W3"]).reshape(16, 11, 128, 16, 128).transpose(0, 1, 4, 3, 2))
    w2 = np.ascontiguousarray(
        np.asarray(i["L0_ffn_W2"]).reshape(16, 16, 128, 11, 128).transpose(0, 1, 4, 3, 2))
    ws1 = np.ascontiguousarray(
        np.asarray(i["L0_ffn_Ws1"]).reshape(22, 128, 16, 128).transpose(0, 3, 2, 1))
    ws3 = np.ascontiguousarray(
        np.asarray(i["L0_ffn_Ws3"]).reshape(22, 128, 16, 128).transpose(0, 3, 2, 1))
    ws2 = np.ascontiguousarray(
        np.asarray(i["L0_ffn_Ws2"]).reshape(16, 128, 22, 128).transpose(0, 3, 2, 1))
    gwh = np.ascontiguousarray(
        np.asarray(i["L0_ffn_gate_w"]).reshape(16, 16, 128).transpose(2, 1, 0))
    return dict(win=win, wmod=wmod, w1=w1, w3=w3, w2=w2, ws1=ws1, ws3=ws3,
                ws2=ws2, gw=gwh)


def kernel(**inputs):
    x = np.asarray(inputs["L0_attn_x"])
    cfull = np.asarray(inputs["L0_c"])
    apar = np.asarray(inputs["L0_attn_a_param"])

    shared = _prep_weights(inputs)
    ident = np.eye(128, dtype=np.float32)
    selsh = np.zeros((16, 2048), dtype=np.float32)
    for c in range(16):
        selsh[c, 128 * c:128 * (c + 1)] = 1.0
    apcol = np.ascontiguousarray(apar.reshape(128, 1).astype(np.float32))

    in_maps = []
    for core in range(8):
        b, half = core // 2, core % 2
        full = np.zeros((2048, 2048), dtype=np.float32)
        if half == 1:
            full[0:1024] = x[b, 0:1024]
        full[1024:2048] = x[b, half * 1024:(half + 1) * 1024]
        xf = np.ascontiguousarray(full.T)
        cvec = np.ascontiguousarray(cfull[b].reshape(16, 128).T)
        mc = np.full((128, 1), 1.0 if half == 1 else 0.0, dtype=np.float32)
        m = dict(shared)
        m.update(xf=xf, cvec=cvec, mcol=mc, apcol=apcol, ident=ident, sels=selsh)
        in_maps.append(m)

    nc = build_nc()
    res = run_bass_kernel_spmd(nc, in_maps, core_ids=list(range(8)))

    out = np.empty((4, 2048, 2048), dtype=np.float32)
    for core in range(8):
        b, half = core // 2, core % 2
        out[b, half * 1024:(half + 1) * 1024, :] = res.results[core]["out"].T
    return out
